# version 19
# baseline (speedup 1.0000x reference)
import math
from contextlib import ExitStack

import numpy as np

import concourse.bass as bass
import concourse.mybir as mybir
from concourse.bass_utils import run_bass_kernel_spmd

F32 = mybir.dt.float32
BF16 = mybir.dt.bfloat16
AF = mybir.ActivationFunctionType
ALU = mybir.AluOpType
AX = mybir.AxisListType

S = 2048
H = 4096
NHB = H // 128
NQH = 32
NKV = 8
D = 128
CH = 1024
NCH = S // CH
SUB = 256
NSUB = CH // SUB
KVO = NKV * D
NOB = NQH + NKV + NKV
NGRP = NOB // 2
TT = 512
NTILE = S // TT
INV_SQRT_D = 1.0 / math.sqrt(D)
EXP_SHIFT = -10.0

ENG_COMPUTE = ("pe", "sc", "ve")
ENG_ALL = ("pe", "sc", "ve", "sy", "gd")


class _Buf:
    __slots__ = ("name", "last_w", "readers", "no_waw")

    def __init__(self, name, no_waw=False):
        self.name = name
        self.last_w = None
        self.readers = {}
        self.no_waw = no_waw


class _Sched:

    def __init__(self):
        self.q = {e: [] for e in ENG_ALL}
        self.cum = {e: 0 for e in ENG_COMPUTE}
        self.streams = {}
        self.satisfied = {e: {} for e in ENG_ALL}
        self.psum_fence = []
        self.nops = 0

    def _record(self, eng, fn, reads, writes, stream, mark):
        waits = {}
        sat = self.satisfied[eng]

        def need(tok, what):
            if tok is None:
                return
            sem, val = tok
            if sem == "eng:pe" and eng == "pe":
                return
            if val is None:
                raise AssertionError(f"dep on unmarked op ({what}) eng={eng}")
            if sat.get(sem, 0) >= val:
                return
            waits[sem] = max(waits.get(sem, 0), val)

        for b in reads:
            need(b.last_w, f"RAW {b.name}")
        for b in writes:
            if not (b.no_waw and stream is not None):
                need(b.last_w, f"WAW {b.name}")
            for tok in b.readers.values():
                need(tok, f"WAR {b.name}")

        for sem, val in waits.items():
            sat[sem] = max(sat.get(sem, 0), val)

        if stream is not None:
            tot = self.streams.get(stream, 0) + 16
            self.streams[stream] = tot
            tok = ("st:" + stream, tot)
            inc = ("st:" + stream, 16)
        elif mark:
            self.cum[eng] += 1
            tok = ("eng:" + eng, self.cum[eng])
            inc = ("eng:" + eng, 1)
        else:
            tok = ("eng:" + eng, None)
            inc = None

        for b in reads:
            b.readers[eng] = tok
        for b in writes:
            b.last_w = tok
            b.readers = {}

        self.q[eng].append((waits, fn, inc))
        return tok

    def op(self, eng, fn, reads=(), writes=(), mark=True):
        return self._record(eng, fn, reads, writes, None, mark)

    def dma(self, eng, stream, fn, reads=(), writes=()):
        return self._record(eng, fn, reads, writes, stream, True)

    def barrier(self):
        targets = {}
        for e in ENG_COMPUTE:
            if self.cum[e]:
                targets["eng:" + e] = self.cum[e]
        for st, tot in self.streams.items():
            targets["st:" + st] = tot
        for e in ENG_ALL:
            sat = self.satisfied[e]
            waits = {}
            for sem, val in targets.items():
                if e == "pe" and sem == "eng:pe":
                    pass
                if sat.get(sem, 0) >= val:
                    continue
                waits[sem] = val
                sat[sem] = val
            if waits:
                self.nops += 1
                self.q[e].append((waits, lambda eng: eng.nop(), None))

    def fence_psum(self, *bufs):
        for b in bufs:
            if b.last_w is not None:
                self.psum_fence.append(b.last_w)
            self.psum_fence.extend(b.readers.values())

    def new_psum_buf(self, name):
        b = _Buf(name)
        for i, tok in enumerate(self.psum_fence):
            b.readers["_f%d" % i] = tok
        return b

    def emit(self, nc, block):
        sems = {}
        stack = ExitStack()
        for e in ENG_COMPUTE:
            sems["eng:" + e] = stack.enter_context(nc.semaphore("s_" + e))
        for st in self.streams:
            sems["st:" + st] = stack.enter_context(nc.semaphore("s_" + st))
        engmap = {"pe": block.tensor, "sc": block.scalar, "ve": block.vector,
                  "sy": block.sync, "gd": block.gpsimd}

        def run(e):
            def body(eng):
                for waits, fn, inc in self.q[e]:
                    for sem, val in sorted(waits.items()):
                        eng.wait_ge(sems[sem], val)
                    inst = fn(eng)
                    if inc is not None:
                        inst.then_inc(sems[inc[0]], inc[1])
            engmap[e](body)

        for e in ENG_ALL:
            run(e)
        return stack


def build_nc(debug=False):
    nc = bass.Bass("TRN2", target_bir_lowering=False)
    sch = _Sched()

    xT = nc.dram_tensor("xT", [H, S], F32, kind="ExternalInput")
    Wblk = nc.dram_tensor("Wblk", [NOB, 128, 4096], F32, kind="ExternalInput")
    Wo_blk = nc.dram_tensor("Wo_blk", [32, 128, 4096], F32, kind="ExternalInput")
    wr_in = nc.dram_tensor("wr_blk", [128, 128], F32, kind="ExternalInput")
    a_in = nc.dram_tensor("a_blk", [128, 2048], F32, kind="ExternalInput")
    bc_in = nc.dram_tensor("bcat", [64, 4096], F32, kind="ExternalInput")
    cn_in = nc.dram_tensor("consts", [128, 576], F32, kind="ExternalInput")
    cf_in = nc.dram_tensor("cf32", [128, 8], F32, kind="ExternalInput")
    tg_in = nc.dram_tensor("trig", [128, 4096], F32, kind="ExternalInput")
    outT = nc.dram_tensor("outT", [H, S], F32, kind="ExternalOutput")
    skind = "ExternalOutput" if debug else "Internal"
    QTs = nc.dram_tensor("QTs", [NQH * D, S], BF16, kind=skind)
    KTs = nc.dram_tensor("KTs", [KVO, S], BF16, kind=skind)
    Vs = nc.dram_tensor("Vs", [S, KVO], BF16, kind=skind)
    if debug:
        xnT_dump = nc.dram_tensor("xnT_dump", [H, S], BF16, kind="ExternalOutput")
        mid2_dump = nc.dram_tensor("mid2_dump", [64, S], BF16, kind="ExternalOutput")
        OT_dump = nc.dram_tensor("OT_dump", [NQH * D, S], BF16, kind="ExternalOutput")

    dbuf = {}
    for t in ("xT", "QTs", "KTs", "Vs", "outT"):
        dbuf[t] = _Buf(t, no_waw=True)

    st = ExitStack()

    const_sb = st.enter_context(nc.sbuf_tensor("const_sb", [128, 576], BF16))
    cf_sb = st.enter_context(nc.sbuf_tensor("cf_sb", [128, 8], F32))
    trig_sb = st.enter_context(nc.sbuf_tensor("trig_sb", [128, 4096], BF16))
    wr_sb = st.enter_context(nc.sbuf_tensor("wr_sb", [128, 128], BF16))
    at_sb = st.enter_context(nc.sbuf_tensor("at_sb", [128, 2048], BF16))
    mid2_sb = st.enter_context(nc.sbuf_tensor("mid2_sb", [64, S], BF16))

    b_const = _Buf("const"); b_trig = _Buf("trig"); b_wr = _Buf("wr")
    b_at = _Buf("at"); b_bcat = _Buf("bcat"); b_mid2 = _Buf("mid2")

    EPS_c = cf_sb[0:1, 0:1]
    NHALF_c = cf_sb[0:1, 1:2]
    ISD_c = cf_sb[:, 2:3]
    SHIFT_c = cf_sb[:, 3:4]
    INVH_c = cf_sb[0:1, 4:5]
    NEG1_c = cf_sb[0:1, 5:6]
    NEGBIG_c = cf_sb[:, 6:7]

    ident = const_sb[:, 0:128]
    prot = const_sb[:, 128:256]
    mask01 = const_sb[:, 256:384]
    e4 = const_sb[0:4, 384:448]
    ones_col = const_sb[:, 448:449]
    ones_row = const_sb[0:1, 448:576]

    sch.dma("gd", "c0", lambda g, o=const_sb[:], i=cn_in.ap(): g.dma_start(o, i), writes=[b_const])
    sch.dma("gd", "c1", lambda g, o=trig_sb[:], i=tg_in.ap(): g.dma_start(o, i), writes=[b_trig])
    sch.dma("gd", "c2", lambda g, o=wr_sb[:], i=wr_in.ap(): g.dma_start(o, i), writes=[b_wr])
    sch.dma("gd", "c3", lambda g, o=at_sb[:], i=a_in.ap(): g.dma_start(o, i), writes=[b_at])
    b_cf = _Buf("cf")
    sch.dma("sy", "c5", lambda s, o=cf_sb[:], i=cf_in.ap(): s.dma_start(o, i), writes=[b_cf])

    with ExitStack() as p1:
        xnT = p1.enter_context(nc.sbuf_tensor("xnT", [128, NHB * CH], BF16))
        xsq = [p1.enter_context(nc.sbuf_tensor(f"xsq{i}", [128, 2 * NHB * SUB], BF16)) for i in range(2)]
        wg = [p1.enter_context(nc.sbuf_tensor(f"wg{i}", [128, 4096], BF16)) for i in range(3)]
        stat = [p1.enter_context(nc.sbuf_tensor("stat0", [1, 6 * SUB], F32))] * 2
        statb = [p1.enter_context(nc.sbuf_tensor("statb0", [1, 2 * SUB], BF16))] * 2
        rbnb = [p1.enter_context(nc.sbuf_tensor(f"rbnb{i}", [128, 2 * SUB], BF16)) for i in range(2)]
        qraw = [p1.enter_context(nc.sbuf_tensor(f"qraw{i}", [128, CH], BF16)) for i in range(2)]
        rotb = [p1.enter_context(nc.sbuf_tensor(f"rotb{i}", [128, CH], BF16)) for i in range(2)]
        qsp = [p1.enter_context(nc.sbuf_tensor(f"qsp{i}", [128, CH], BF16)) for i in range(2)]
        vstag = p1.enter_context(nc.sbuf_tensor("vstag", [128, CH], BF16))
        lgb = p1.enter_context(nc.sbuf_tensor("lgb", [4, CH], BF16))
        lg_nat = p1.enter_context(nc.sbuf_tensor("lg_nat", [128, (CH // 128) * 4], F32))
        gt_sc = p1.enter_context(nc.sbuf_tensor("gt_sc", [128, 112], F32))
        gt_bfn = p1.enter_context(nc.sbuf_tensor("gt_bfn", [128, (CH // 128) * 4], BF16))
        gatesT = p1.enter_context(nc.sbuf_tensor("gatesT", [4, CH], BF16))
        g4_bf = p1.enter_context(nc.sbuf_tensor("g4_bf", [64, CH], BF16))

        b_xnT = _Buf("xnT")
        b_xa = [_Buf(f"xa{i}") for i in range(2)]
        b_xs = [_Buf(f"xs{i}") for i in range(2)]
        b_wg = [_Buf(f"wg{i}") for i in range(3)]
        b_stat = [_Buf("stat0")] * 2
        b_statb = [_Buf("statb0")] * 2
        b_rbnb = [_Buf(f"rbnb{i}") for i in range(2)]
        b_qraw = [_Buf(f"qraw{i}") for i in range(2)]
        b_rotb = [_Buf(f"rotb{i}") for i in range(2)]
        b_qsp = [_Buf(f"qsp{i}") for i in range(2)]
        b_vstag = _Buf("vstag")
        b_lgb = _Buf("lgb"); b_lgnat = _Buf("lg_nat"); b_gtsc = _Buf("gt_sc")
        b_gtbfn = _Buf("gt_bfn"); b_gatesT = _Buf("gatesT"); b_g4 = _Buf("g4_bf")

        NS = NHB * SUB

        def rec_xch(sub, c0):
            sl_ = sub % 2
            t0c = c0 + sub * SUB
            in_ap = bass.AP(xT, t0c, [[S, 128], [128 * S, NHB], [1, SUB]])
            sch.dma("gd", f"xch{sl_}",
                    lambda g, o=xsq[sl_][:, 0:NS], i=in_ap: g.dma_start(o, i),
                    reads=[dbuf["xT"]], writes=[b_xa[sl_]])

        def rec_sq(sub):
            sl = sub % 2
            sch.op("sc", lambda s, o=xsq[sl][:, NS:2 * NS], i=xsq[sl][:, 0:NS]:
                   s.activation(o, i, AF.Square),
                   reads=[b_xa[sl]], writes=[b_xs[sl]])

        def rec_mms(sub, prp_t, b_prp_t):
            sl = sub % 2
            for hb in range(NHB):
                rhs = bass.AP(xsq[sl], hb * SUB, [[2 * NS, 128], [NS, 2], [1, SUB]])
                sch.op("pe", lambda t, o=prp_t[0:1, 0:2 * SUB], l=ones_col, r=rhs,
                       st=(hb == 0), sp=(hb == NHB - 1):
                       t.matmul(o, l, r, start=st, stop=sp),
                       reads=[b_const, b_xa[sl], b_xs[sl]], writes=[b_prp_t])

        def rec_chain(sub, prp_t, b_prp_t):
            sl = sub % 2
            stp, b_stp = stat[sl], b_stat[sl]
            sbp, b_sbp = statb[sl], b_statb[sl]
            m = stp[0:1, 0:SUB]; ex2 = stp[0:1, SUB:2 * SUB]
            var = stp[0:1, 2 * SUB:3 * SUB]; lnv = stp[0:1, 5 * SUB:6 * SUB]
            sch.op("ve", lambda v, o=stp[0:1, 0:2 * SUB], i=prp_t[0:1, 0:2 * SUB]:
                   v.tensor_scalar(o, i, INVH_c, None, ALU.mult),
                   reads=[b_prp_t, b_cf], writes=[b_stp])
            sch.op("ve", lambda v, o=var, a=m, bb=m:
                   v.tensor_tensor(o, a, bb, ALU.mult), reads=[b_stp], writes=[b_stp])
            sch.op("ve", lambda v, o=var, a=ex2, bb=var:
                   v.tensor_tensor(o, a, bb, ALU.subtract), reads=[b_stp], writes=[b_stp])
            sch.op("sc", lambda s, o=lnv, i=var: s.activation(o, i, AF.Ln, bias=EPS_c),
                   reads=[b_stp, b_cf], writes=[b_stp])
            sch.op("sc", lambda s, o=sbp[0:1, 0:SUB], i=lnv:
                   s.activation(o, i, AF.Exp, scale=NHALF_c),
                   reads=[b_stp, b_cf], writes=[b_sbp])
            sch.op("ve", lambda v, o=sbp[0:1, SUB:2 * SUB], i0=m, i1=sbp[0:1, 0:SUB]:
                   v.scalar_tensor_tensor(o, i0, NEG1_c, i1, ALU.mult, ALU.mult),
                   reads=[b_stp, b_sbp, b_cf], writes=[b_sbp])
            sch.op("pe", lambda t, o=prp_t[:], l=ones_row, r=sbp[0:1, 0:2 * SUB]:
                   t.matmul(o, l, r, start=True, stop=True),
                   reads=[b_const, b_sbp], writes=[b_prp_t])
            sch.op("sc", lambda s, o=rbnb[sl][:], i=prp_t[:]: s.copy(o, i),
                   reads=[b_prp_t], writes=[b_rbnb[sl]])

        def rec_stats(sub, c0, prp_t, b_prp_t):
            rec_sq(sub)
            rec_mms(sub, prp_t, b_prp_t)
            rec_chain(sub, prp_t, b_prp_t)

        for c in range(NCH):
            c0 = c * CH

            with ExitStack() as pa:
                prp = [pa.enter_context(nc.psum_tensor(f"prp{c}_{i}", [128, 2 * SUB], F32)) for i in range(2)]
                b_prp = [sch.new_psum_buf(f"prp{i}") for i in range(2)]

                if c == 0:
                    rec_xch(0, c0); rec_xch(1, c0)
                for sub in range(NSUB):
                    sl = sub % 2
                    if not (c == 1 and sub < 2):
                        rec_stats(sub, c0, prp[sl], b_prp[sl])
                    rb_bc = bass.AP(rbnb[sl], 0, [[2 * SUB, 128], [0, 4], [1, SUB]])
                    nb_bc = bass.AP(rbnb[sl], SUB, [[2 * SUB, 128], [0, 4], [1, SUB]])
                    for hb in range(0, NHB, 4):
                        xb4 = bass.AP(xsq[sl], hb * SUB, [[2 * NS, 128], [SUB, 4], [1, SUB]])
                        dst4 = bass.AP(xnT, hb * CH + sub * SUB, [[NHB * CH, 128], [CH, 4], [1, SUB]])
                        t4 = bass.AP(xsq[sl], NS + hb * SUB, [[2 * NS, 128], [SUB, 4], [1, SUB]])
                        sch.op("ve", lambda v, o=t4, a=xb4, bb=rb_bc:
                               v.tensor_tensor(o, a, bb, ALU.mult),
                               reads=[b_xa[sl], b_rbnb[sl]], writes=[b_xs[sl]])
                        sch.op("ve", lambda v, o=dst4, a=t4, bb=nb_bc:
                               v.tensor_tensor(o, a, bb, ALU.add),
                               reads=[b_xs[sl], b_rbnb[sl]], writes=[b_xnT])
                    if sub + 2 < NSUB:
                        rec_xch(sub + 2, c0)
                sch.fence_psum(b_prp[0], b_prp[1])
                if debug:
                    for hb in range(NHB):
                        dap = bass.AP(xnT_dump, hb * 128 * S + c0, [[S, 128], [1, CH]])
                        sch.dma("sy", "dbgX", lambda s, o=dap, i=xnT[:, hb * CH:(hb + 1) * CH]:
                                s.dma_start(o, i), reads=[b_xnT], writes=[_Buf("dmpx")])

            ob_order = list(range(40, 48)) + list(range(32, 40)) + list(range(32))

            def rec_wdma(obi):
                ob = ob_order[obi]
                sl = obi % 3
                wap = bass.AP(Wblk, ob * 128 * 4096, [[4096, 128], [1, 4096]])
                sch.dma("gd", f"wg{sl}",
                        lambda g, o=wg[sl][:], i=wap: g.dma_start(o, i),
                        writes=[b_wg[sl]])

            with ExitStack() as pv:
                svp_l = [pv.enter_context(nc.psum_tensor(f"svp{c}_{i}", [128, CH], F32)) for i in range(2)]
                tp = [pv.enter_context(nc.psum_tensor(f"tp{c}_{i}", [128, 128], BF16)) for i in range(2)]
                b_svp_l = [sch.new_psum_buf(f"svp{i}") for i in range(2)]
                b_tp = [sch.new_psum_buf(f"tp{i}") for i in range(2)]
                rec_wdma(0); rec_wdma(1); rec_wdma(2)
                for obi in range(8):
                    ob = ob_order[obi]; sl = obi % 3
                    gv = ob - 40
                    svp, b_svp = svp_l[gv % 2], b_svp_l[gv % 2]
                    for hb in range(NHB):
                        for ck in range(CH // 512):
                            sch.op("pe", lambda t, o=svp[:, ck * 512:(ck + 1) * 512],
                                   l=wg[sl][:, hb * 128:(hb + 1) * 128],
                                   r=xnT[:, hb * CH + ck * 512: hb * CH + (ck + 1) * 512],
                                   st=(hb == 0), sp=(hb == NHB - 1):
                                   t.matmul(o, l, r, start=st, stop=sp),
                                   reads=[b_wg[sl], b_xnT], writes=[b_svp],
                                   mark=(hb == NHB - 1))
                    sch.op("sc", lambda s, o=qraw[0][:], i=svp[:]: s.copy(o, i),
                           reads=[b_svp], writes=[b_qraw[0]])
                    for sb in range(CH // 128):
                        sch.op("pe", lambda t, o=tp[sb % 2][:],
                               i=qraw[0][:, sb * 128:(sb + 1) * 128], idn=ident:
                               t.transpose(o, i, idn),
                               reads=[b_qraw[0], b_const], writes=[b_tp[sb % 2]])
                        sch.op("ve", lambda v, o=vstag[:, sb * 128:(sb + 1) * 128],
                               i=tp[sb % 2][:]: v.tensor_copy(o, i),
                               reads=[b_tp[sb % 2]], writes=[b_vstag])
                    out_ap = bass.AP(Vs, c0 * KVO + gv * 128,
                                     [[KVO, 128], [128 * KVO, CH // 128], [1, 128]])
                    sch.dma("sy", "spV",
                            lambda s, o=out_ap, i=vstag[:]: s.dma_start(o, i),
                            reads=[b_vstag], writes=[dbuf["Vs"]])
                    if obi + 3 < len(ob_order):
                        rec_wdma(obi + 3)
                sch.fence_psum(b_svp_l[0], b_svp_l[1], b_tp[0], b_tp[1])

            with ExitStack() as pb:
                lgp = pb.enter_context(nc.psum_tensor(f"lgp{c}", [4, CH], F32))
                ltp = [pb.enter_context(nc.psum_tensor(f"ltp{c}_{i}", [128, 4], BF16)) for i in range(2)]
                gtp = [pb.enter_context(nc.psum_tensor(f"gtp{c}_{i}", [4, 128], BF16)) for i in range(2)]
                b_lgp = sch.new_psum_buf("lgp")
                b_ltp = [sch.new_psum_buf(f"ltp{i}") for i in range(2)]
                b_gtp = [sch.new_psum_buf(f"gtp{i}") for i in range(2)]

                for hb in range(NHB):
                    for ck in range(CH // 512):
                        sch.op("pe", lambda t, o=lgp[0:4, ck * 512:(ck + 1) * 512],
                               l=wr_sb[:, hb * 4:(hb + 1) * 4],
                               r=xnT[:, hb * CH + ck * 512: hb * CH + (ck + 1) * 512],
                               st=(hb == 0), sp=(hb == NHB - 1):
                               t.matmul(o, l, r, start=st, stop=sp),
                               reads=[b_wr, b_xnT], writes=[b_lgp],
                               mark=(hb == NHB - 1))
                sch.op("sc", lambda s, o=lgb[:], i=lgp[0:4, :]: s.copy(o, i),
                       reads=[b_lgp], writes=[b_lgb])
                ntb = CH // 128
                for tb in range(ntb):
                    sch.op("pe", lambda t, o=ltp[tb % 2][:], i=lgb[0:4, tb * 128:(tb + 1) * 128],
                           idn=ident[0:4, 0:4]: t.transpose(o, i, idn),
                           reads=[b_lgb, b_const], writes=[b_ltp[tb % 2]])
                    sch.op("ve", lambda v, o=lg_nat[:, tb * 4:(tb + 1) * 4], i=ltp[tb % 2][:]:
                           v.tensor_copy(o, i), reads=[b_ltp[tb % 2]], writes=[b_lgnat])
                G4 = ntb * 4
                GSC = 3 * G4 + 2 * ntb

                def a3(t_, off, stride):
                    return bass.AP(t_, off, [[stride, 128], [4, ntb], [1, 4]])

                def abc(t_, off, stride):
                    return bass.AP(t_, off, [[stride, 128], [1, ntb], [0, 4]])

                dd = gt_sc[:, 0:G4]; e1 = gt_sc[:, G4:2 * G4]; msk = gt_sc[:, 2 * G4:3 * G4]
                m1 = gt_sc[:, 3 * G4:3 * G4 + ntb]; t2 = gt_sc[:, 3 * G4 + ntb:GSC]
                dd3 = a3(gt_sc, 0, GSC); msk3 = a3(gt_sc, 2 * G4, GSC)
                m1bc = abc(gt_sc, 3 * G4, GSC); t2bc = abc(gt_sc, 3 * G4 + ntb, GSC)
                sch.op("ve", lambda v, o=m1, i=a3(lg_nat, 0, G4): v.reduce_max(o, i, AX.X),
                       reads=[b_lgnat], writes=[b_gtsc])
                sch.op("ve", lambda v, o=dd3, a=a3(lg_nat, 0, G4), bb=m1bc:
                       v.tensor_tensor(o, a, bb, ALU.subtract),
                       reads=[b_lgnat, b_gtsc], writes=[b_gtsc])
                sch.op("ve", lambda v, o=e1, i=dd:
                       v.tensor_single_scalar(o, i, 0.0, ALU.is_equal),
                       reads=[b_gtsc], writes=[b_gtsc])
                sch.op("ve", lambda v, o=msk, i0=e1, i1=dd:
                       v.scalar_tensor_tensor(o, i0, NEGBIG_c, i1, ALU.mult, ALU.add),
                       reads=[b_gtsc, b_cf], writes=[b_gtsc])
                sch.op("ve", lambda v, o=t2, i=msk3: v.reduce_max(o, i, AX.X),
                       reads=[b_gtsc], writes=[b_gtsc])
                sch.op("ve", lambda v, o=msk3, a=dd3, bb=t2bc:
                       v.tensor_tensor(o, a, bb, ALU.is_ge),
                       reads=[b_gtsc], writes=[b_gtsc])
                sch.op("sc", lambda s, o=e1, i=dd: s.activation(o, i, AF.Exp),
                       reads=[b_gtsc], writes=[b_gtsc])
                sch.op("ve", lambda v, o=dd, a=e1, bb=msk:
                       v.tensor_tensor(o, a, bb, ALU.mult),
                       reads=[b_gtsc], writes=[b_gtsc])
                sch.op("ve", lambda v, o=m1, i=dd3: v.reduce_sum(o, i, AX.X),
                       reads=[b_gtsc], writes=[b_gtsc])
                sch.op("ve", lambda v, o=t2, i=m1: v.reciprocal(o, i),
                       reads=[b_gtsc], writes=[b_gtsc])
                sch.op("ve", lambda v, o=a3(gt_bfn, 0, G4), a=dd3, bb=t2bc:
                       v.tensor_tensor(o, a, bb, ALU.mult),
                       reads=[b_gtsc], writes=[b_gtbfn])
                for tb in range(ntb):
                    sch.op("pe", lambda t, o=gtp[tb % 2][:], i=gt_bfn[:, tb * 4:(tb + 1) * 4],
                           idn=ident: t.transpose(o, i, idn),
                           reads=[b_gtbfn, b_const], writes=[b_gtp[tb % 2]])
                    sch.op("sc", lambda s, o=gatesT[0:4, tb * 128:(tb + 1) * 128], i=gtp[tb % 2][:]:
                           s.copy(o, i), reads=[b_gtp[tb % 2]], writes=[b_gatesT])
                sch.fence_psum(b_lgp, b_ltp[0], b_ltp[1], b_gtp[0], b_gtp[1])

            with ExitStack() as pb2:
                g4p = pb2.enter_context(nc.psum_tensor(f"g4p{c}", [64, CH], F32))
                midp = pb2.enter_context(nc.psum_tensor(f"midp{c}", [64, CH], F32))
                b_g4p = sch.new_psum_buf("g4p")
                b_midp = sch.new_psum_buf("midp")
                for ck in range(CH // 512):
                    sch.op("pe", lambda t, o=g4p[0:64, ck * 512:(ck + 1) * 512],
                           l=e4, r=gatesT[0:4, ck * 512:(ck + 1) * 512]:
                           t.matmul(o, l, r, start=True, stop=True),
                           reads=[b_const, b_gatesT], writes=[b_g4p])
                sch.op("sc", lambda s, o=g4_bf[:], i=g4p[0:64, :]: s.copy(o, i),
                       reads=[b_g4p], writes=[b_g4])
                for hb in range(NHB):
                    for ck in range(CH // 512):
                        sch.op("pe", lambda t, o=midp[0:64, ck * 512:(ck + 1) * 512],
                               l=at_sb[:, hb * 64:(hb + 1) * 64],
                               r=xnT[:, hb * CH + ck * 512: hb * CH + (ck + 1) * 512],
                               st=(hb == 0), sp=(hb == NHB - 1):
                               t.matmul(o, l, r, start=st, stop=sp),
                               reads=[b_at, b_xnT], writes=[b_midp],
                               mark=(hb == NHB - 1))
                sch.op("ve", lambda v, o=mid2_sb[0:64, c0:c0 + CH], a=midp[0:64, :], bb=g4_bf[:]:
                       v.tensor_tensor(o, a, bb, ALU.mult),
                       reads=[b_midp, b_g4], writes=[b_mid2])
                sch.fence_psum(b_g4p, b_midp)


            with ExitStack() as pqk:
                sqp = [pqk.enter_context(nc.psum_tensor(f"sqp{c}_{i}", [128, CH], F32)) for i in range(2)]
                rot_ps = pqk.enter_context(nc.psum_tensor(f"rotp{c}", [128, CH], F32))
                b_sqp = [sch.new_psum_buf(f"sqp{i}") for i in range(2)]
                b_rotp = sch.new_psum_buf("rotp")
                if c == 0:
                    pre_ps = [pqk.enter_context(nc.psum_tensor(f"pre{i}", [128, 2 * SUB], F32)) for i in range(2)]
                    b_pre = [sch.new_psum_buf(f"pre{i}") for i in range(2)]
                pend_rot = []
                for obi in range(8, len(ob_order)):
                    ob = ob_order[obi]; sl = obi % 3
                    ps = obi % 2
                    for hb in range(NHB):
                        for ck in range(CH // 512):
                            sch.op("pe", lambda t, o=sqp[ps][:, ck * 512:(ck + 1) * 512],
                                   l=wg[sl][:, hb * 128:(hb + 1) * 128],
                                   r=xnT[:, hb * CH + ck * 512: hb * CH + (ck + 1) * 512],
                                   st=(hb == 0), sp=(hb == NHB - 1):
                                   t.matmul(o, l, r, start=st, stop=sp),
                                   reads=[b_wg[sl], b_xnT], writes=[b_sqp[ps]],
                                   mark=(hb == NHB - 1))
                    sch.op("sc", lambda s, o=qraw[ps][:], i=sqp[ps][:]: s.copy(o, i),
                           reads=[b_sqp[ps]], writes=[b_qraw[ps]])

                    def rot_chain(ps=ps, ob=ob, c0=c0):
                        for ck in range(CH // 512):
                            sch.op("pe", lambda t, o=rot_ps[:, ck * 512:(ck + 1) * 512],
                                   l=prot, r=qraw[ps][:, ck * 512:(ck + 1) * 512]:
                                   t.matmul(o, l, r, start=True, stop=True),
                                   reads=[b_const, b_qraw[ps]], writes=[b_rotp],
                                   mark=(ck == CH // 512 - 1))
                        sch.op("sc", lambda s, o=rotb[ps][:], i=rot_ps[:]: s.copy(o, i),
                               reads=[b_rotp], writes=[b_rotb[ps]])
                        sch.op("ve", lambda v, o=rotb[ps][:], a=rotb[ps][:], bb=trig_sb[:, 2048 + c0:2048 + c0 + CH]:
                               v.tensor_tensor(o, a, bb, ALU.mult),
                               reads=[b_rotb[ps], b_trig], writes=[b_rotb[ps]])
                        sch.op("ve", lambda v, o=qsp[ps][:], a=qraw[ps][:], bb=trig_sb[:, c0:c0 + CH]:
                               v.tensor_tensor(o, a, bb, ALU.mult),
                               reads=[b_qraw[ps], b_trig], writes=[b_qsp[ps]])
                        sch.op("ve", lambda v, o=qsp[ps][:], a=qsp[ps][:], bb=rotb[ps][:]:
                               v.tensor_tensor(o, a, bb, ALU.add),
                               reads=[b_rotb[ps], b_qsp[ps]], writes=[b_qsp[ps]])
                        if ob < 32:
                            dst, dn = QTs, "QTs"
                            off = ob * 128 * S + c0
                        else:
                            dst, dn = KTs, "KTs"
                            off = (ob - 32) * 128 * S + c0
                        out_ap = bass.AP(dst, off, [[S, 128], [1, CH]])
                        sch.dma("sy", f"spQ{ps}",
                                lambda s, o=out_ap, i=qsp[ps][:]: s.dma_start(o, i),
                                reads=[b_qsp[ps]], writes=[dbuf[dn]])
                    while pend_rot:
                        pend_rot.pop(0)()
                    pend_rot.append(rot_chain)
                    if obi + 3 < len(ob_order):
                        rec_wdma(obi + 3)
                    if c == 0:
                        if obi == 36: rec_xch(0, CH)
                        elif obi == 38: rec_xch(1, CH)
                        elif obi == 39: rec_sq(0)
                        elif obi == 41: rec_sq(1)
                        elif obi == 42: rec_mms(0, pre_ps[0], b_pre[0])
                        elif obi == 43: rec_mms(1, pre_ps[1], b_pre[1])
                        elif obi == 45: rec_chain(0, pre_ps[0], b_pre[0])
                        elif obi == 46: rec_chain(1, pre_ps[1], b_pre[1])
                while pend_rot:
                    pend_rot.pop(0)()
                sch.fence_psum(b_sqp[0], b_sqp[1], b_rotp, *( [b_pre[0], b_pre[1]] if c == 0 else []))

        if debug:
            dap = bass.AP(mid2_dump, 0, [[S, 64], [1, S]])
            sch.dma("sy", "dbgM", lambda s, o=dap, i=mid2_sb[0:64, :]:
                    s.dma_start(o, i), reads=[b_mid2], writes=[_Buf("dmpm")])
        sch.barrier()

    with ExitStack() as p23:
        OTr = p23.enter_context(nc.sbuf_tensor("OTr", [128, NQH * S], BF16))
        bcat_sb = p23.enter_context(nc.sbuf_tensor("bcat_sb", [64, 4096], BF16))
        wo = [p23.enter_context(nc.sbuf_tensor(f"wo{i}", [128, 4096], BF16)) for i in range(2)]
        b_OTr = _Buf("OTr")
        b_wo = [_Buf(f"wo{i}") for i in range(2)]
        sch.dma("gd", "c4", lambda g, o=bcat_sb[:], i=bc_in.ap(): g.dma_start(o, i), writes=[b_bcat])

        def rec_wo_dma(nb):
            sl = nb % 2
            woap = bass.AP(Wo_blk, nb * 128 * 4096, [[4096, 128], [1, 4096]])
            sch.dma("gd", f"wo{sl}", lambda g, o=wo[sl][:], i=woap:
                    g.dma_start(o, i), writes=[b_wo[sl]])

        rec_wo_dma(0); rec_wo_dma(1)

        with ExitStack() as p2:
            ktg = [p2.enter_context(nc.sbuf_tensor(f"ktg{i}", [128, S], BF16)) for i in range(2)]
            vg = [p2.enter_context(nc.sbuf_tensor(f"vg{i}", [128, S], BF16)) for i in range(2)]
            qt = [p2.enter_context(nc.sbuf_tensor(f"qt{i}", [128, TT], BF16)) for i in range(4)]
            pt_sb = [p2.enter_context(nc.sbuf_tensor(f"pt{i}", [128, 2 * TT], BF16)) for i in range(4)]
            recf = p2.enter_context(nc.sbuf_tensor("recf", [1, TT], F32))
            recb = [p2.enter_context(nc.sbuf_tensor(f"recb{i}", [1, TT], BF16)) for i in range(2)]
            rb_sb = p2.enter_context(nc.sbuf_tensor("rb_sb", [128, TT], F32))

            pt_ps = [p2.enter_context(nc.psum_tensor(f"ptp{i}", [128, 2 * TT], F32)) for i in range(2)]
            ot_ps = [p2.enter_context(nc.psum_tensor(f"otp{i}", [128, TT], F32)) for i in range(2)]
            denrb_ps = [p2.enter_context(nc.psum_tensor(f"denrb{i}", [128, TT], F32)) for i in range(2)]

            b_ktg = [_Buf(f"ktg{i}") for i in range(2)]
            b_vg = [_Buf(f"vg{i}") for i in range(2)]
            b_qt = [_Buf(f"qt{i}") for i in range(4)]
            b_ptsb = [_Buf(f"pt{i}") for i in range(4)]
            b_recf = _Buf("recf"); b_recb = [_Buf(f"recb{i}") for i in range(2)]; b_rbsb = _Buf("rb_sb")
            b_ptps = [sch.new_psum_buf(f"ptp{i}") for i in range(2)]
            b_otps = [sch.new_psum_buf(f"otp{i}") for i in range(2)]
            b_denrb = [sch.new_psum_buf(f"denrb{i}") for i in range(2)]

            def rec_kv_dma(g):
                sl = g % 2
                kin = bass.AP(KTs, g * 128 * S, [[S, 128], [1, S]])
                sch.dma("sy", f"ktg{sl}", lambda s, o=ktg[sl][:], i=kin: s.dma_start(o, i),
                        reads=[dbuf["KTs"]], writes=[b_ktg[sl]])
                vin = bass.AP(Vs, g * 128, [[KVO, 128], [128 * KVO, S // 128], [1, 128]])
                sch.dma("sy", f"vg{sl}", lambda s, o=vg[sl][:], i=vin: s.dma_start(o, i),
                        reads=[dbuf["Vs"]], writes=[b_vg[sl]])

            rec_kv_dma(0)
            pending_tail = []
            for g in range(NKV):
                gsl = g % 2
                if g + 1 < NKV:
                    rec_kv_dma(g + 1)
                iters = [(g * 4 + 2 * hp + st_, tile)
                         for hp in range(2) for tile in range(NTILE) for st_ in range(2)]

                def rec_qt(i):
                    hh, tt_ = iters[i]
                    qsl = i % 4
                    qin = bass.AP(QTs, hh * 128 * S + tt_ * TT, [[S, 128], [1, TT]])
                    sch.dma("sy", f"qt{qsl}", lambda s, o=qt[qsl][:], i=qin: s.dma_start(o, i),
                            reads=[dbuf["QTs"]], writes=[b_qt[qsl]])

                rec_qt(0); rec_qt(1)
                it = 0
                for hp in range(2):
                    for tile in range(NTILE):
                        heads = (g * 4 + 2 * hp, g * 4 + 2 * hp + 1)
                        qsl2 = (it % 4, (it + 1) % 4)
                        if it + 2 < len(iters):
                            rec_qt(it + 2)
                        if it + 3 < len(iters):
                            rec_qt(it + 3)
                        t0 = tile * TT
                        nsb = t0 // 128 + 4
                        npair = nsb // 2

                        def rec_scores(j, st_):
                            par = st_
                            for half in range(2):
                                sb = 2 * j + half
                                vis0 = max(0, sb * 128 - t0)
                                sch.op("pe", lambda t,
                                       o=pt_ps[par][:, half * TT + vis0: (half + 1) * TT],
                                       l=ktg[gsl][:, sb * 128:(sb + 1) * 128],
                                       r=qt[qsl2[st_]][:, vis0:TT]:
                                       t.matmul(o, l, r, start=True, stop=True),
                                       reads=[b_ktg[gsl], b_qt[qsl2[st_]]], writes=[b_ptps[par]])

                        def rec_exp(j, st_, slot):
                            par = st_
                            vis = [max(0, (2 * j + half) * 128 - t0) for half in range(2)]
                            if vis[0] == 0 and vis[1] == 0:
                                sch.op("sc", lambda s, o=pt_sb[slot][:, 0:2 * TT],
                                       i=pt_ps[par][:, 0:2 * TT]:
                                       s.activation(o, i, AF.Exp, bias=SHIFT_c, scale=ISD_c),
                                       reads=[b_ptps[par], b_cf], writes=[b_ptsb[slot]])
                            else:
                                for half in range(2):
                                    v0 = half * TT + vis[half]
                                    sch.op("sc", lambda s, o=pt_sb[slot][:, v0:(half + 1) * TT],
                                           i=pt_ps[par][:, v0:(half + 1) * TT]:
                                           s.activation(o, i, AF.Exp, bias=SHIFT_c, scale=ISD_c),
                                           reads=[b_ptps[par], b_cf], writes=[b_ptsb[slot]])
                            for half in range(2):
                                sb = 2 * j + half
                                if sb * 128 >= t0:
                                    v0 = half * TT + vis[half]
                                    sch.op("ve", lambda v, o=pt_sb[slot][:, v0:v0 + 128],
                                           a=pt_sb[slot][:, v0:v0 + 128], bb=mask01:
                                           v.tensor_tensor(o, a, bb, ALU.mult),
                                           reads=[b_ptsb[slot], b_const], writes=[b_ptsb[slot]])

                        def rec_pv(j, st_, slot):
                            vis = [max(0, (2 * j + half) * 128 - t0) for half in range(2)]
                            first = (j == 0)
                            last = (j == npair - 1)
                            for half in range(2):
                                sb = 2 * j + half
                                v0 = vis[half]
                                sch.op("pe", lambda t, o=ot_ps[st_][:, v0:TT],
                                       l=vg[gsl][:, sb * 128:(sb + 1) * 128],
                                       r=pt_sb[slot][:, half * TT + v0:(half + 1) * TT],
                                       st=(first and half == 0), sp=(last and half == 1):
                                       t.matmul(o, l, r, start=st, stop=sp),
                                       reads=[b_vg[gsl], b_ptsb[slot]], writes=[b_otps[st_]])
                            for half in range(2):
                                sb = 2 * j + half
                                v0 = vis[half]
                                sch.op("pe", lambda t, o=denrb_ps[st_][0:1, v0:TT],
                                       l=ones_col,
                                       r=pt_sb[slot][:, half * TT + v0:(half + 1) * TT],
                                       st=(first and half == 0), sp=(last and half == 1):
                                       t.matmul(o, l, r, start=st, stop=sp),
                                       reads=[b_const, b_ptsb[slot]], writes=[b_denrb[st_]])

                        def slot_of(j, st_):
                            return (2 * j + st_) % 4

                        for j in range(npair):
                            rec_scores(j, 0)
                            rec_scores(j, 1)
                            while pending_tail:
                                pending_tail.pop(0)()
                            rec_exp(j, 0, slot_of(j, 0))
                            rec_exp(j, 1, slot_of(j, 1))
                            if j >= 1:
                                rec_pv(j - 1, 0, slot_of(j - 1, 0))
                                rec_pv(j - 1, 1, slot_of(j - 1, 1))
                        rec_pv(npair - 1, 0, slot_of(npair - 1, 0))
                        rec_pv(npair - 1, 1, slot_of(npair - 1, 1))

                        def rec_tail(h_=heads, t0=t0):
                            for st_ in range(2):
                                rf = recf[0:1, 0:TT]
                                sch.op("sc", lambda s, o=rf, i=denrb_ps[st_][0:1, :], st_=st_:
                                       s.activation(o, i, AF.Ln), reads=[b_denrb[st_]], writes=[b_recf])
                                sch.op("sc", lambda s, o=recb[st_][0:1, :], i=rf, st_=st_:
                                       s.activation(o, i, AF.Exp, scale=NEG1_c),
                                       reads=[b_recf, b_cf], writes=[b_recb[st_]])
                                sch.op("pe", lambda t, o=denrb_ps[st_][:], l=ones_row,
                                       r=recb[st_][0:1, :], st_=st_:
                                       t.matmul(o, l, r, start=True, stop=True),
                                       reads=[b_const, b_recb[st_]], writes=[b_denrb[st_]])
                                sch.op("ve", lambda v, o=rb_sb[:], i=denrb_ps[st_][:], st_=st_:
                                       v.tensor_copy(o, i),
                                       reads=[b_denrb[st_]], writes=[b_rbsb])
                                sch.op("ve", lambda v, o=OTr[:, h_[st_] * S + t0: h_[st_] * S + t0 + TT],
                                       a=ot_ps[st_][:], bb=rb_sb[:], st_=st_:
                                       v.tensor_tensor(o, a, bb, ALU.mult),
                                       reads=[b_otps[st_], b_rbsb], writes=[b_OTr])
                        pending_tail.append(rec_tail)
                        it += 2
            while pending_tail:
                pending_tail.pop(0)()
            sch.fence_psum(*(b_ptps + b_otps + b_denrb))
            if debug:
                for h in range(NQH):
                    oap = bass.AP(OT_dump, h * 128 * S, [[S, 128], [1, S]])
                    sch.dma("sy", "dbgO", lambda s, o=oap, i=OTr[:, h * S:(h + 1) * S]:
                            s.dma_start(o, i), reads=[b_OTr], writes=[_Buf("dump")])

        sch.barrier()

        with ExitStack() as p3:
            xt3 = [p3.enter_context(nc.sbuf_tensor(f"xt3{i}", [128, S], F32)) for i in range(2)]
            gt3 = [p3.enter_context(nc.sbuf_tensor(f"gt3{i}", [128, CH], BF16)) for i in range(2)]
            outst = [p3.enter_context(nc.sbuf_tensor(f"outst{i}", [128, CH], F32)) for i in range(2)]
            st_ps = [p3.enter_context(nc.psum_tensor(f"stp{i}", [128, CH], F32)) for i in range(2)]
            dl_ps = [p3.enter_context(nc.psum_tensor(f"dlp{i}", [128, CH], F32)) for i in range(2)]
            b_xt3 = [_Buf(f"xt3{i}") for i in range(2)]
            b_gt3 = [_Buf(f"gt3{i}") for i in range(2)]
            b_outst = [_Buf(f"outst{i}") for i in range(2)]
            b_stps = [sch.new_psum_buf(f"stp{i}") for i in range(2)]
            b_dlps = [sch.new_psum_buf(f"dlp{i}") for i in range(2)]

            def rec_p3_dma(nb, skip_wo=False):
                if not skip_wo:
                    rec_wo_dma(nb)
                sl = nb % 2
                xin = bass.AP(xT, nb * 128 * S, [[S, 128], [1, S]])
                sch.dma("sy", f"xt3{sl}", lambda s, o=xt3[sl][:], i=xin: s.dma_start(o, i),
                        reads=[dbuf["xT"]], writes=[b_xt3[sl]])

            rec_p3_dma(0, skip_wo=True); rec_p3_dma(1, skip_wo=True)
            it3 = 0
            for nb in range(32):
                sl = nb % 2
                for tc in range(S // CH):
                    ps = it3 % 2
                    it3 += 1
                    for ob in range(32):
                        for ck in range(CH // 512):
                            sch.op("pe", lambda t, o=st_ps[ps][:, ck * 512:(ck + 1) * 512],
                                   l=wo[sl][:, ob * 128:(ob + 1) * 128],
                                   r=OTr[:, ob * S + tc * CH + ck * 512: ob * S + tc * CH + (ck + 1) * 512],
                                   st=(ob == 0), sp=(ob == 31):
                                   t.matmul(o, l, r, start=st, stop=sp),
                                   reads=[b_wo[sl], b_OTr], writes=[b_stps[ps]],
                                   mark=(ob == 31))
                    for ck in range(CH // 512):
                        sch.op("pe", lambda t, o=dl_ps[ps][:, ck * 512:(ck + 1) * 512],
                               l=bcat_sb[0:64, nb * 128:(nb + 1) * 128],
                               r=mid2_sb[0:64, tc * CH + ck * 512: tc * CH + (ck + 1) * 512]:
                               t.matmul(o, l, r, start=True, stop=True),
                               reads=[b_bcat, b_mid2], writes=[b_dlps[ps]],
                               mark=(ck == CH // 512 - 1))
                    sch.op("sc", lambda s, o=gt3[ps][:], i=dl_ps[ps][:]:
                           s.activation(o, i, AF.Silu),
                           reads=[b_dlps[ps]], writes=[b_gt3[ps]])
                    sch.op("ve", lambda v, o=outst[ps][:], a=st_ps[ps][:], bb=gt3[ps][:]:
                           v.tensor_tensor(o, a, bb, ALU.mult),
                           reads=[b_stps[ps], b_gt3[ps]], writes=[b_outst[ps]])
                    sch.op("ve", lambda v, o=outst[ps][:], a=outst[ps][:],
                           bb=xt3[sl][:, tc * CH:(tc + 1) * CH]:
                           v.tensor_tensor(o, a, bb, ALU.add),
                           reads=[b_outst[ps], b_xt3[sl]], writes=[b_outst[ps]])
                    oap = bass.AP(outT, nb * 128 * S + tc * CH, [[S, 128], [1, CH]])
                    sch.dma("sy", f"so{ps}", lambda s, o=oap, i=outst[ps][:]:
                            s.dma_start(o, i), reads=[b_outst[ps]], writes=[dbuf["outT"]])
                if nb + 2 < 32:
                    rec_p3_dma(nb + 2)
            sch.fence_psum(*(b_stps + b_dlps))

    sch.barrier()

    with nc.Block() as block:
        semstack = sch.emit(nc, block)
    semstack.close()
    st.close()
    return nc



def _host_shared(Wq, Wk, Wv, Wo, A, Bm, Wr):
    f32 = np.float32
    WcatT = np.concatenate([Wq.T, Wk.T, Wv.T], axis=1).astype(f32)
    Wblk = np.ascontiguousarray(
        WcatT.reshape(32, 128, 48, 128).transpose(2, 1, 0, 3).reshape(48, 128, 4096))
    WoT = Wo.T.astype(f32)
    Wo_blk = np.ascontiguousarray(
        WoT.reshape(32, 128, 32, 128).transpose(2, 1, 0, 3).reshape(32, 128, 4096))
    wr_blk = np.ascontiguousarray(
        Wr.T.astype(f32).reshape(32, 128, 4).transpose(1, 0, 2).reshape(128, 128))
    a_blk = np.ascontiguousarray(
        A.transpose(2, 0, 1).reshape(4096, 64).astype(f32)
        .reshape(32, 128, 64).transpose(1, 0, 2).reshape(128, 2048))
    bcat = np.ascontiguousarray(Bm.transpose(0, 2, 1).reshape(64, 4096).astype(f32))

    consts = np.zeros((128, 576), f32)
    consts[:, 0:128] = np.eye(128, dtype=f32)
    prot = np.zeros((128, 128), f32)
    for d in range(64):
        prot[d + 64, d] = -1.0
        prot[d, d + 64] = 1.0
    consts[:, 128:256] = prot
    m01 = np.zeros((128, 128), f32)
    for s_ in range(128):
        m01[s_, s_:] = 1.0
    consts[:, 256:384] = m01
    e4 = np.zeros((128, 64), f32)
    for e in range(4):
        e4[e, e * 16:(e + 1) * 16] = 4.0
    consts[:, 384:448] = e4
    consts[:, 448:576] = 1.0
    cf32 = np.zeros((128, 8), f32)
    cf32[:, 0] = 1e-5
    cf32[:, 1] = -0.5
    cf32[:, 2] = 1.0 / math.sqrt(128.0)
    cf32[:, 3] = -10.0
    cf32[:, 4] = 1.0 / 4096.0
    cf32[:, 5] = -1.0
    cf32[:, 6] = -1e9

    freqs = np.exp(-np.arange(0, 128, 2, dtype=f32) * (math.log(10000.0) / 128)).astype(f32)
    ang = np.arange(S, dtype=f32)[:, None] * freqs[None, :]
    cosT = np.cos(ang).T.astype(f32)
    sinT = np.sin(ang).T.astype(f32)
    trig = np.zeros((128, 4096), f32)
    trig[0:64, 0:S] = cosT
    trig[64:128, 0:S] = cosT
    trig[0:64, 2048:2048 + S] = sinT
    trig[64:128, 2048:2048 + S] = sinT
    return dict(Wblk=Wblk, Wo_blk=Wo_blk, wr_blk=wr_blk, a_blk=a_blk,
                bcat=bcat, consts=consts, cf32=cf32, trig=trig)


def kernel(L0_attn_hidden_states, L0_attn_Wq, L0_attn_Wk, L0_attn_Wv, L0_attn_Wo,
           L0_ffn_A, L0_ffn_B, L0_ffn_Wr):
    x = np.asarray(L0_attn_hidden_states, dtype=np.float32)
    shared = _host_shared(np.asarray(L0_attn_Wq), np.asarray(L0_attn_Wk),
                          np.asarray(L0_attn_Wv), np.asarray(L0_attn_Wo),
                          np.asarray(L0_ffn_A), np.asarray(L0_ffn_B),
                          np.asarray(L0_ffn_Wr))
    nc = build_nc()
    in_maps = []
    for i in range(8):
        m = {"xT": np.ascontiguousarray(x[i].T)}
        m.update(shared)
        in_maps.append(m)
    res = run_bass_kernel_spmd(nc, in_maps, core_ids=list(range(8)))
    out = np.empty((8, S, H), dtype=np.float32)
    for i in range(8):
        out[i] = res.results[i]["outT"].T
    return out


# version 20
# speedup vs baseline: 1.1915x; 1.1915x over previous
import math
from contextlib import ExitStack

import numpy as np

import concourse.bass as bass
import concourse.mybir as mybir
from concourse.bass_utils import run_bass_kernel_spmd

F32 = mybir.dt.float32
BF16 = mybir.dt.bfloat16
AF = mybir.ActivationFunctionType
ALU = mybir.AluOpType
AX = mybir.AxisListType

S = 2048
H = 4096
NHB = H // 128
NQH = 32
NKV = 8
D = 128
CH = 1024
NCH = S // CH
SUB = 128
NSUB = CH // SUB
KVO = NKV * D
NOB = NQH + NKV + NKV
NGRP = NOB // 2
TT = 512
NTILE = S // TT
INV_SQRT_D = 1.0 / math.sqrt(D)
EXP_SHIFT = -10.0

ENG_COMPUTE = ("pe", "sc", "ve")
ENG_ALL = ("pe", "sc", "ve", "sy", "gd")


class _Buf:
    __slots__ = ("name", "last_w", "readers", "no_waw")

    def __init__(self, name, no_waw=False):
        self.name = name
        self.last_w = None
        self.readers = {}
        self.no_waw = no_waw


class _Sched:

    def __init__(self):
        self.q = {e: [] for e in ENG_ALL}
        self.cum = {e: 0 for e in ENG_COMPUTE}
        self.streams = {}
        self.satisfied = {e: {} for e in ENG_ALL}
        self.psum_fence = []
        self.nops = 0

    def _record(self, eng, fn, reads, writes, stream, mark):
        waits = {}
        sat = self.satisfied[eng]

        def need(tok, what):
            if tok is None:
                return
            sem, val = tok
            if sem == "eng:pe" and eng == "pe":
                return
            if val is None:
                raise AssertionError(f"dep on unmarked op ({what}) eng={eng}")
            if sat.get(sem, 0) >= val:
                return
            waits[sem] = max(waits.get(sem, 0), val)

        for b in reads:
            need(b.last_w, f"RAW {b.name}")
        for b in writes:
            if not (b.no_waw and stream is not None):
                need(b.last_w, f"WAW {b.name}")
            for tok in b.readers.values():
                need(tok, f"WAR {b.name}")

        for sem, val in waits.items():
            sat[sem] = max(sat.get(sem, 0), val)

        if stream is not None:
            tot = self.streams.get(stream, 0) + 16
            self.streams[stream] = tot
            tok = ("st:" + stream, tot)
            inc = ("st:" + stream, 16)
        elif mark:
            self.cum[eng] += 1
            tok = ("eng:" + eng, self.cum[eng])
            inc = ("eng:" + eng, 1)
        else:
            tok = ("eng:" + eng, None)
            inc = None

        for b in reads:
            b.readers[eng] = tok
        for b in writes:
            b.last_w = tok
            b.readers = {}

        self.q[eng].append((waits, fn, inc))
        return tok

    def op(self, eng, fn, reads=(), writes=(), mark=True):
        return self._record(eng, fn, reads, writes, None, mark)

    def dma(self, eng, stream, fn, reads=(), writes=()):
        return self._record(eng, fn, reads, writes, stream, True)

    def barrier(self):
        targets = {}
        for e in ENG_COMPUTE:
            if self.cum[e]:
                targets["eng:" + e] = self.cum[e]
        for st, tot in self.streams.items():
            targets["st:" + st] = tot
        for e in ENG_ALL:
            sat = self.satisfied[e]
            waits = {}
            for sem, val in targets.items():
                if e == "pe" and sem == "eng:pe":
                    pass
                if sat.get(sem, 0) >= val:
                    continue
                waits[sem] = val
                sat[sem] = val
            if waits:
                self.nops += 1
                self.q[e].append((waits, lambda eng: eng.nop(), None))

    def fence_psum(self, *bufs):
        for b in bufs:
            if b.last_w is not None:
                self.psum_fence.append(b.last_w)
            self.psum_fence.extend(b.readers.values())

    def new_psum_buf(self, name):
        b = _Buf(name)
        for i, tok in enumerate(self.psum_fence):
            b.readers["_f%d" % i] = tok
        return b

    def emit(self, nc, block):
        sems = {}
        stack = ExitStack()
        for e in ENG_COMPUTE:
            sems["eng:" + e] = stack.enter_context(nc.semaphore("s_" + e))
        for st in self.streams:
            sems["st:" + st] = stack.enter_context(nc.semaphore("s_" + st))
        engmap = {"pe": block.tensor, "sc": block.scalar, "ve": block.vector,
                  "sy": block.sync, "gd": block.gpsimd}

        def run(e):
            def body(eng):
                for waits, fn, inc in self.q[e]:
                    for sem, val in sorted(waits.items()):
                        eng.wait_ge(sems[sem], val)
                    inst = fn(eng)
                    if inc is not None:
                        inst.then_inc(sems[inc[0]], inc[1])
            engmap[e](body)

        for e in ENG_ALL:
            run(e)
        return stack


def build_nc(debug=False):
    nc = bass.Bass("TRN2", target_bir_lowering=False)
    sch = _Sched()

    xT = nc.dram_tensor("xT", [H, S], F32, kind="ExternalInput")
    Wblk = nc.dram_tensor("Wblk", [NOB, 128, 4096], F32, kind="ExternalInput")
    Wo_blk = nc.dram_tensor("Wo_blk", [32, 128, 4096], F32, kind="ExternalInput")
    wr_in = nc.dram_tensor("wr_blk", [128, 128], F32, kind="ExternalInput")
    a_in = nc.dram_tensor("a_blk", [128, 2048], F32, kind="ExternalInput")
    bc_in = nc.dram_tensor("bcat", [64, 4096], F32, kind="ExternalInput")
    cn_in = nc.dram_tensor("consts", [128, 576], F32, kind="ExternalInput")
    cf_in = nc.dram_tensor("cf32", [128, 8], F32, kind="ExternalInput")
    tg_in = nc.dram_tensor("trig", [128, 4096], F32, kind="ExternalInput")
    outT = nc.dram_tensor("outT", [H, S], F32, kind="ExternalOutput")
    skind = "ExternalOutput" if debug else "Internal"
    QTs = nc.dram_tensor("QTs", [NQH * D, S], BF16, kind=skind)
    KTs = nc.dram_tensor("KTs", [KVO, S], BF16, kind=skind)
    Vs = nc.dram_tensor("Vs", [S, KVO], BF16, kind=skind)
    if debug:
        xnT_dump = nc.dram_tensor("xnT_dump", [H, S], BF16, kind="ExternalOutput")
        mid2_dump = nc.dram_tensor("mid2_dump", [64, S], BF16, kind="ExternalOutput")
        OT_dump = nc.dram_tensor("OT_dump", [NQH * D, S], BF16, kind="ExternalOutput")

    dbuf = {}
    for t in ("xT", "QTs", "KTs", "Vs", "outT"):
        dbuf[t] = _Buf(t, no_waw=True)

    st = ExitStack()

    const_sb = st.enter_context(nc.sbuf_tensor("const_sb", [128, 576], BF16))
    cf_sb = st.enter_context(nc.sbuf_tensor("cf_sb", [128, 8], F32))
    trig_sb = st.enter_context(nc.sbuf_tensor("trig_sb", [128, 4096], BF16))
    wr_sb = st.enter_context(nc.sbuf_tensor("wr_sb", [128, 128], BF16))
    at_sb = st.enter_context(nc.sbuf_tensor("at_sb", [128, 2048], BF16))
    mid2_sb = st.enter_context(nc.sbuf_tensor("mid2_sb", [64, S], BF16))

    b_const = _Buf("const"); b_trig = _Buf("trig"); b_wr = _Buf("wr")
    b_at = _Buf("at"); b_bcat = _Buf("bcat"); b_mid2 = _Buf("mid2")

    EPS_c = cf_sb[0:1, 0:1]
    NHALF_c = cf_sb[0:1, 1:2]
    ISD_c = cf_sb[:, 2:3]
    SHIFT_c = cf_sb[:, 3:4]
    INVH_c = cf_sb[0:1, 4:5]
    NEG1_c = cf_sb[0:1, 5:6]
    NEGBIG_c = cf_sb[:, 6:7]

    ident = const_sb[:, 0:128]
    prot = const_sb[:, 128:256]
    mask01 = const_sb[:, 256:384]
    e4 = const_sb[0:4, 384:448]
    ones_col = const_sb[:, 448:449]
    ones_row = const_sb[0:1, 448:576]

    sch.dma("gd", "c0", lambda g, o=const_sb[:], i=cn_in.ap(): g.dma_start(o, i), writes=[b_const])
    sch.dma("gd", "c1", lambda g, o=trig_sb[:], i=tg_in.ap(): g.dma_start(o, i), writes=[b_trig])
    sch.dma("gd", "c2", lambda g, o=wr_sb[:], i=wr_in.ap(): g.dma_start(o, i), writes=[b_wr])
    sch.dma("gd", "c3", lambda g, o=at_sb[:], i=a_in.ap(): g.dma_start(o, i), writes=[b_at])
    b_cf = _Buf("cf")
    sch.dma("sy", "c5", lambda s, o=cf_sb[:], i=cf_in.ap(): s.dma_start(o, i), writes=[b_cf])

    with ExitStack() as p1:
        xnT = p1.enter_context(nc.sbuf_tensor("xnT", [128, NHB * CH], BF16))
        xsq = [p1.enter_context(nc.sbuf_tensor(f"xsq{i}", [128, 2 * NHB * SUB], BF16)) for i in range(3)]
        wg = [p1.enter_context(nc.sbuf_tensor(f"wg{i}", [128, 4096], BF16)) for i in range(3)]
        stat = [p1.enter_context(nc.sbuf_tensor("stat0", [1, 6 * SUB], F32))] * 2
        statb = [p1.enter_context(nc.sbuf_tensor("statb0", [1, 2 * SUB], BF16))] * 2
        rbnb = [p1.enter_context(nc.sbuf_tensor(f"rbnb{i}", [128, 2 * SUB], BF16)) for i in range(2)]
        qraw = [p1.enter_context(nc.sbuf_tensor(f"qraw{i}", [128, CH], BF16)) for i in range(2)]
        rotb = [p1.enter_context(nc.sbuf_tensor(f"rotb{i}", [128, CH], BF16)) for i in range(2)]
        qsp = [p1.enter_context(nc.sbuf_tensor(f"qsp{i}", [128, CH], BF16)) for i in range(2)]
        vstag = p1.enter_context(nc.sbuf_tensor("vstag", [128, CH], BF16))
        lgb = p1.enter_context(nc.sbuf_tensor("lgb", [4, CH], BF16))
        lg_nat = p1.enter_context(nc.sbuf_tensor("lg_nat", [128, (CH // 128) * 4], F32))
        gt_sc = p1.enter_context(nc.sbuf_tensor("gt_sc", [128, 112], F32))
        gt_bfn = p1.enter_context(nc.sbuf_tensor("gt_bfn", [128, (CH // 128) * 4], BF16))
        gatesT = p1.enter_context(nc.sbuf_tensor("gatesT", [4, CH], BF16))
        g4_bf = p1.enter_context(nc.sbuf_tensor("g4_bf", [64, CH], BF16))

        b_xnT = _Buf("xnT")
        b_xa = [_Buf(f"xa{i}") for i in range(3)]
        b_xs = [_Buf(f"xs{i}") for i in range(3)]
        b_wg = [_Buf(f"wg{i}") for i in range(3)]
        b_stat = [_Buf("stat0")] * 2
        b_statb = [_Buf("statb0")] * 2
        b_rbnb = [_Buf(f"rbnb{i}") for i in range(2)]
        b_qraw = [_Buf(f"qraw{i}") for i in range(2)]
        b_rotb = [_Buf(f"rotb{i}") for i in range(2)]
        b_qsp = [_Buf(f"qsp{i}") for i in range(2)]
        b_vstag = _Buf("vstag")
        b_lgb = _Buf("lgb"); b_lgnat = _Buf("lg_nat"); b_gtsc = _Buf("gt_sc")
        b_gtbfn = _Buf("gt_bfn"); b_gatesT = _Buf("gatesT"); b_g4 = _Buf("g4_bf")

        NS = NHB * SUB

        def rec_xch(sub, c0):
            sl_ = sub % 3
            t0c = c0 + sub * SUB
            in_ap = bass.AP(xT, t0c, [[S, 128], [128 * S, NHB], [1, SUB]])
            sch.dma("gd", f"xch{sl_}",
                    lambda g, o=xsq[sl_][:, 0:NS], i=in_ap: g.dma_start(o, i),
                    reads=[dbuf["xT"]], writes=[b_xa[sl_]])

        def rec_sq(sub):
            sl = sub % 3
            sch.op("sc", lambda s, o=xsq[sl][:, NS:2 * NS], i=xsq[sl][:, 0:NS]:
                   s.activation(o, i, AF.Square),
                   reads=[b_xa[sl]], writes=[b_xs[sl]])

        def rec_mms(sub, prp_t, b_prp_t):
            sl = sub % 3
            for hb in range(NHB):
                rhs = bass.AP(xsq[sl], hb * SUB, [[2 * NS, 128], [NS, 2], [1, SUB]])
                sch.op("pe", lambda t, o=prp_t[0:1, 0:2 * SUB], l=ones_col, r=rhs,
                       st=(hb == 0), sp=(hb == NHB - 1):
                       t.matmul(o, l, r, start=st, stop=sp),
                       reads=[b_const, b_xa[sl], b_xs[sl]], writes=[b_prp_t])

        def rec_chain(sub, prp_t, b_prp_t):
            sl = sub % 3
            stp, b_stp = stat[sl % 2], b_stat[sl % 2]
            sbp, b_sbp = statb[sl % 2], b_statb[sl % 2]
            m = stp[0:1, 0:SUB]; ex2 = stp[0:1, SUB:2 * SUB]
            var = stp[0:1, 2 * SUB:3 * SUB]; lnv = stp[0:1, 5 * SUB:6 * SUB]
            sch.op("ve", lambda v, o=stp[0:1, 0:2 * SUB], i=prp_t[0:1, 0:2 * SUB]:
                   v.tensor_scalar(o, i, INVH_c, None, ALU.mult),
                   reads=[b_prp_t, b_cf], writes=[b_stp])
            sch.op("ve", lambda v, o=var, a=m, bb=m:
                   v.tensor_tensor(o, a, bb, ALU.mult), reads=[b_stp], writes=[b_stp])
            sch.op("ve", lambda v, o=var, a=ex2, bb=var:
                   v.tensor_tensor(o, a, bb, ALU.subtract), reads=[b_stp], writes=[b_stp])
            sch.op("sc", lambda s, o=lnv, i=var: s.activation(o, i, AF.Ln, bias=EPS_c),
                   reads=[b_stp, b_cf], writes=[b_stp])
            sch.op("sc", lambda s, o=sbp[0:1, 0:SUB], i=lnv:
                   s.activation(o, i, AF.Exp, scale=NHALF_c),
                   reads=[b_stp, b_cf], writes=[b_sbp])
            sch.op("ve", lambda v, o=sbp[0:1, SUB:2 * SUB], i0=m, i1=sbp[0:1, 0:SUB]:
                   v.scalar_tensor_tensor(o, i0, NEG1_c, i1, ALU.mult, ALU.mult),
                   reads=[b_stp, b_sbp, b_cf], writes=[b_sbp])
            sch.op("pe", lambda t, o=prp_t[:], l=ones_row, r=sbp[0:1, 0:2 * SUB]:
                   t.matmul(o, l, r, start=True, stop=True),
                   reads=[b_const, b_sbp], writes=[b_prp_t])
            sch.op("sc", lambda s, o=rbnb[sl % 2][:], i=prp_t[:]: s.copy(o, i),
                   reads=[b_prp_t], writes=[b_rbnb[sl % 2]])

        def rec_stats(sub, c0, prp_t, b_prp_t):
            rec_sq(sub)
            rec_mms(sub, prp_t, b_prp_t)
            rec_chain(sub, prp_t, b_prp_t)

        for c in range(NCH):
            c0 = c * CH

            with ExitStack() as pa:
                prp = [pa.enter_context(nc.psum_tensor(f"prp{c}_{i}", [128, 2 * SUB], F32)) for i in range(2)]
                b_prp = [sch.new_psum_buf(f"prp{i}") for i in range(2)]

                if c == 0:
                    rec_xch(0, c0); rec_xch(1, c0); rec_xch(2, c0)
                for sub in range(NSUB):
                    sl = sub % 3
                    if not (c == 1 and sub < 2):
                        rec_stats(sub, c0, prp[sub % 2], b_prp[sub % 2])
                    rb_bc = bass.AP(rbnb[sl % 2], 0, [[2 * SUB, 128], [0, 4], [1, SUB]])
                    nb_bc = bass.AP(rbnb[sl % 2], SUB, [[2 * SUB, 128], [0, 4], [1, SUB]])
                    for hb in range(0, NHB, 4):
                        xb4 = bass.AP(xsq[sl], hb * SUB, [[2 * NS, 128], [SUB, 4], [1, SUB]])
                        dst4 = bass.AP(xnT, hb * CH + sub * SUB, [[NHB * CH, 128], [CH, 4], [1, SUB]])
                        t4 = bass.AP(xsq[sl], NS + hb * SUB, [[2 * NS, 128], [SUB, 4], [1, SUB]])
                        sch.op("ve", lambda v, o=t4, a=xb4, bb=rb_bc:
                               v.tensor_tensor(o, a, bb, ALU.mult),
                               reads=[b_xa[sl], b_rbnb[sl % 2]], writes=[b_xs[sl]])
                        sch.op("ve", lambda v, o=dst4, a=t4, bb=nb_bc:
                               v.tensor_tensor(o, a, bb, ALU.add),
                               reads=[b_xs[sl], b_rbnb[sl % 2]], writes=[b_xnT])
                    if sub + 3 < NSUB:
                        rec_xch(sub + 3, c0)
                sch.fence_psum(b_prp[0], b_prp[1])
                if debug:
                    for hb in range(NHB):
                        dap = bass.AP(xnT_dump, hb * 128 * S + c0, [[S, 128], [1, CH]])
                        sch.dma("sy", "dbgX", lambda s, o=dap, i=xnT[:, hb * CH:(hb + 1) * CH]:
                                s.dma_start(o, i), reads=[b_xnT], writes=[_Buf("dmpx")])

            ob_order = list(range(40, 48)) + list(range(32, 40)) + list(range(32))

            def rec_wdma(obi):
                ob = ob_order[obi]
                sl = obi % 3
                wap = bass.AP(Wblk, ob * 128 * 4096, [[4096, 128], [1, 4096]])
                sch.dma("gd", f"wg{sl}",
                        lambda g, o=wg[sl][:], i=wap: g.dma_start(o, i),
                        writes=[b_wg[sl]])

            with ExitStack() as pv:
                svp_l = [pv.enter_context(nc.psum_tensor(f"svp{c}_{i}", [128, CH], F32)) for i in range(2)]
                tp = [pv.enter_context(nc.psum_tensor(f"tp{c}_{i}", [128, 128], BF16)) for i in range(2)]
                b_svp_l = [sch.new_psum_buf(f"svp{i}") for i in range(2)]
                b_tp = [sch.new_psum_buf(f"tp{i}") for i in range(2)]
                rec_wdma(0); rec_wdma(1); rec_wdma(2)
                for obi in range(8):
                    ob = ob_order[obi]; sl = obi % 3
                    gv = ob - 40
                    svp, b_svp = svp_l[gv % 2], b_svp_l[gv % 2]
                    for hb in range(NHB):
                        for ck in range(CH // 512):
                            sch.op("pe", lambda t, o=svp[:, ck * 512:(ck + 1) * 512],
                                   l=wg[sl][:, hb * 128:(hb + 1) * 128],
                                   r=xnT[:, hb * CH + ck * 512: hb * CH + (ck + 1) * 512],
                                   st=(hb == 0), sp=(hb == NHB - 1):
                                   t.matmul(o, l, r, start=st, stop=sp),
                                   reads=[b_wg[sl], b_xnT], writes=[b_svp],
                                   mark=(hb == NHB - 1))
                    sch.op("sc", lambda s, o=qraw[0][:], i=svp[:]: s.copy(o, i),
                           reads=[b_svp], writes=[b_qraw[0]])
                    for sb in range(CH // 128):
                        sch.op("pe", lambda t, o=tp[sb % 2][:],
                               i=qraw[0][:, sb * 128:(sb + 1) * 128], idn=ident:
                               t.transpose(o, i, idn),
                               reads=[b_qraw[0], b_const], writes=[b_tp[sb % 2]])
                        sch.op("ve", lambda v, o=vstag[:, sb * 128:(sb + 1) * 128],
                               i=tp[sb % 2][:]: v.tensor_copy(o, i),
                               reads=[b_tp[sb % 2]], writes=[b_vstag])
                    out_ap = bass.AP(Vs, c0 * KVO + gv * 128,
                                     [[KVO, 128], [128 * KVO, CH // 128], [1, 128]])
                    sch.dma("sy", "spV",
                            lambda s, o=out_ap, i=vstag[:]: s.dma_start(o, i),
                            reads=[b_vstag], writes=[dbuf["Vs"]])
                    if obi + 3 < len(ob_order):
                        rec_wdma(obi + 3)
                sch.fence_psum(b_svp_l[0], b_svp_l[1], b_tp[0], b_tp[1])

            with ExitStack() as pb:
                lgp = pb.enter_context(nc.psum_tensor(f"lgp{c}", [4, CH], F32))
                ltp = [pb.enter_context(nc.psum_tensor(f"ltp{c}_{i}", [128, 4], BF16)) for i in range(2)]
                gtp = [pb.enter_context(nc.psum_tensor(f"gtp{c}_{i}", [4, 128], BF16)) for i in range(2)]
                b_lgp = sch.new_psum_buf("lgp")
                b_ltp = [sch.new_psum_buf(f"ltp{i}") for i in range(2)]
                b_gtp = [sch.new_psum_buf(f"gtp{i}") for i in range(2)]

                for hb in range(NHB):
                    for ck in range(CH // 512):
                        sch.op("pe", lambda t, o=lgp[0:4, ck * 512:(ck + 1) * 512],
                               l=wr_sb[:, hb * 4:(hb + 1) * 4],
                               r=xnT[:, hb * CH + ck * 512: hb * CH + (ck + 1) * 512],
                               st=(hb == 0), sp=(hb == NHB - 1):
                               t.matmul(o, l, r, start=st, stop=sp),
                               reads=[b_wr, b_xnT], writes=[b_lgp],
                               mark=(hb == NHB - 1))
                sch.op("sc", lambda s, o=lgb[:], i=lgp[0:4, :]: s.copy(o, i),
                       reads=[b_lgp], writes=[b_lgb])
                ntb = CH // 128
                for tb in range(ntb):
                    sch.op("pe", lambda t, o=ltp[tb % 2][:], i=lgb[0:4, tb * 128:(tb + 1) * 128],
                           idn=ident[0:4, 0:4]: t.transpose(o, i, idn),
                           reads=[b_lgb, b_const], writes=[b_ltp[tb % 2]])
                    sch.op("ve", lambda v, o=lg_nat[:, tb * 4:(tb + 1) * 4], i=ltp[tb % 2][:]:
                           v.tensor_copy(o, i), reads=[b_ltp[tb % 2]], writes=[b_lgnat])
                G4 = ntb * 4
                GSC = 3 * G4 + 2 * ntb

                def a3(t_, off, stride):
                    return bass.AP(t_, off, [[stride, 128], [4, ntb], [1, 4]])

                def abc(t_, off, stride):
                    return bass.AP(t_, off, [[stride, 128], [1, ntb], [0, 4]])

                dd = gt_sc[:, 0:G4]; e1 = gt_sc[:, G4:2 * G4]; msk = gt_sc[:, 2 * G4:3 * G4]
                m1 = gt_sc[:, 3 * G4:3 * G4 + ntb]; t2 = gt_sc[:, 3 * G4 + ntb:GSC]
                dd3 = a3(gt_sc, 0, GSC); msk3 = a3(gt_sc, 2 * G4, GSC)
                m1bc = abc(gt_sc, 3 * G4, GSC); t2bc = abc(gt_sc, 3 * G4 + ntb, GSC)
                sch.op("ve", lambda v, o=m1, i=a3(lg_nat, 0, G4): v.reduce_max(o, i, AX.X),
                       reads=[b_lgnat], writes=[b_gtsc])
                sch.op("ve", lambda v, o=dd3, a=a3(lg_nat, 0, G4), bb=m1bc:
                       v.tensor_tensor(o, a, bb, ALU.subtract),
                       reads=[b_lgnat, b_gtsc], writes=[b_gtsc])
                sch.op("ve", lambda v, o=e1, i=dd:
                       v.tensor_single_scalar(o, i, 0.0, ALU.is_equal),
                       reads=[b_gtsc], writes=[b_gtsc])
                sch.op("ve", lambda v, o=msk, i0=e1, i1=dd:
                       v.scalar_tensor_tensor(o, i0, NEGBIG_c, i1, ALU.mult, ALU.add),
                       reads=[b_gtsc, b_cf], writes=[b_gtsc])
                sch.op("ve", lambda v, o=t2, i=msk3: v.reduce_max(o, i, AX.X),
                       reads=[b_gtsc], writes=[b_gtsc])
                sch.op("ve", lambda v, o=msk3, a=dd3, bb=t2bc:
                       v.tensor_tensor(o, a, bb, ALU.is_ge),
                       reads=[b_gtsc], writes=[b_gtsc])
                sch.op("sc", lambda s, o=e1, i=dd: s.activation(o, i, AF.Exp),
                       reads=[b_gtsc], writes=[b_gtsc])
                sch.op("ve", lambda v, o=dd, a=e1, bb=msk:
                       v.tensor_tensor(o, a, bb, ALU.mult),
                       reads=[b_gtsc], writes=[b_gtsc])
                sch.op("ve", lambda v, o=m1, i=dd3: v.reduce_sum(o, i, AX.X),
                       reads=[b_gtsc], writes=[b_gtsc])
                sch.op("ve", lambda v, o=t2, i=m1: v.reciprocal(o, i),
                       reads=[b_gtsc], writes=[b_gtsc])
                sch.op("ve", lambda v, o=a3(gt_bfn, 0, G4), a=dd3, bb=t2bc:
                       v.tensor_tensor(o, a, bb, ALU.mult),
                       reads=[b_gtsc], writes=[b_gtbfn])
                for tb in range(ntb):
                    sch.op("pe", lambda t, o=gtp[tb % 2][:], i=gt_bfn[:, tb * 4:(tb + 1) * 4],
                           idn=ident: t.transpose(o, i, idn),
                           reads=[b_gtbfn, b_const], writes=[b_gtp[tb % 2]])
                    sch.op("sc", lambda s, o=gatesT[0:4, tb * 128:(tb + 1) * 128], i=gtp[tb % 2][:]:
                           s.copy(o, i), reads=[b_gtp[tb % 2]], writes=[b_gatesT])
                sch.fence_psum(b_lgp, b_ltp[0], b_ltp[1], b_gtp[0], b_gtp[1])

            with ExitStack() as pb2:
                g4p = pb2.enter_context(nc.psum_tensor(f"g4p{c}", [64, CH], F32))
                midp = pb2.enter_context(nc.psum_tensor(f"midp{c}", [64, CH], F32))
                b_g4p = sch.new_psum_buf("g4p")
                b_midp = sch.new_psum_buf("midp")
                for ck in range(CH // 512):
                    sch.op("pe", lambda t, o=g4p[0:64, ck * 512:(ck + 1) * 512],
                           l=e4, r=gatesT[0:4, ck * 512:(ck + 1) * 512]:
                           t.matmul(o, l, r, start=True, stop=True),
                           reads=[b_const, b_gatesT], writes=[b_g4p])
                sch.op("sc", lambda s, o=g4_bf[:], i=g4p[0:64, :]: s.copy(o, i),
                       reads=[b_g4p], writes=[b_g4])
                for hb in range(NHB):
                    for ck in range(CH // 512):
                        sch.op("pe", lambda t, o=midp[0:64, ck * 512:(ck + 1) * 512],
                               l=at_sb[:, hb * 64:(hb + 1) * 64],
                               r=xnT[:, hb * CH + ck * 512: hb * CH + (ck + 1) * 512],
                               st=(hb == 0), sp=(hb == NHB - 1):
                               t.matmul(o, l, r, start=st, stop=sp),
                               reads=[b_at, b_xnT], writes=[b_midp],
                               mark=(hb == NHB - 1))
                sch.op("ve", lambda v, o=mid2_sb[0:64, c0:c0 + CH], a=midp[0:64, :], bb=g4_bf[:]:
                       v.tensor_tensor(o, a, bb, ALU.mult),
                       reads=[b_midp, b_g4], writes=[b_mid2])
                sch.fence_psum(b_g4p, b_midp)


            with ExitStack() as pqk:
                sqp = [pqk.enter_context(nc.psum_tensor(f"sqp{c}_{i}", [128, CH], F32)) for i in range(2)]
                rot_ps = pqk.enter_context(nc.psum_tensor(f"rotp{c}", [128, CH], F32))
                b_sqp = [sch.new_psum_buf(f"sqp{i}") for i in range(2)]
                b_rotp = sch.new_psum_buf("rotp")
                if c == 0:
                    pre_ps = [pqk.enter_context(nc.psum_tensor(f"pre{i}", [128, 2 * SUB], F32)) for i in range(2)]
                    b_pre = [sch.new_psum_buf(f"pre{i}") for i in range(2)]
                pend_rot = []
                for obi in range(8, len(ob_order)):
                    ob = ob_order[obi]; sl = obi % 3
                    ps = obi % 2
                    for hb in range(NHB):
                        for ck in range(CH // 512):
                            sch.op("pe", lambda t, o=sqp[ps][:, ck * 512:(ck + 1) * 512],
                                   l=wg[sl][:, hb * 128:(hb + 1) * 128],
                                   r=xnT[:, hb * CH + ck * 512: hb * CH + (ck + 1) * 512],
                                   st=(hb == 0), sp=(hb == NHB - 1):
                                   t.matmul(o, l, r, start=st, stop=sp),
                                   reads=[b_wg[sl], b_xnT], writes=[b_sqp[ps]],
                                   mark=(hb == NHB - 1))
                    sch.op("sc", lambda s, o=qraw[ps][:], i=sqp[ps][:]: s.copy(o, i),
                           reads=[b_sqp[ps]], writes=[b_qraw[ps]])

                    def rot_chain(ps=ps, ob=ob, c0=c0):
                        for ck in range(CH // 512):
                            sch.op("pe", lambda t, o=rot_ps[:, ck * 512:(ck + 1) * 512],
                                   l=prot, r=qraw[ps][:, ck * 512:(ck + 1) * 512]:
                                   t.matmul(o, l, r, start=True, stop=True),
                                   reads=[b_const, b_qraw[ps]], writes=[b_rotp],
                                   mark=(ck == CH // 512 - 1))
                        sch.op("sc", lambda s, o=rotb[ps][:], i=rot_ps[:]: s.copy(o, i),
                               reads=[b_rotp], writes=[b_rotb[ps]])
                        sch.op("ve", lambda v, o=rotb[ps][:], a=rotb[ps][:], bb=trig_sb[:, 2048 + c0:2048 + c0 + CH]:
                               v.tensor_tensor(o, a, bb, ALU.mult),
                               reads=[b_rotb[ps], b_trig], writes=[b_rotb[ps]])
                        sch.op("ve", lambda v, o=qsp[ps][:], a=qraw[ps][:], bb=trig_sb[:, c0:c0 + CH]:
                               v.tensor_tensor(o, a, bb, ALU.mult),
                               reads=[b_qraw[ps], b_trig], writes=[b_qsp[ps]])
                        sch.op("ve", lambda v, o=qsp[ps][:], a=qsp[ps][:], bb=rotb[ps][:]:
                               v.tensor_tensor(o, a, bb, ALU.add),
                               reads=[b_rotb[ps], b_qsp[ps]], writes=[b_qsp[ps]])
                        if ob < 32:
                            dst, dn = QTs, "QTs"
                            off = ob * 128 * S + c0
                        else:
                            dst, dn = KTs, "KTs"
                            off = (ob - 32) * 128 * S + c0
                        out_ap = bass.AP(dst, off, [[S, 128], [1, CH]])
                        sch.dma("sy", f"spQ{ps}",
                                lambda s, o=out_ap, i=qsp[ps][:]: s.dma_start(o, i),
                                reads=[b_qsp[ps]], writes=[dbuf[dn]])
                    while pend_rot:
                        pend_rot.pop(0)()
                    pend_rot.append(rot_chain)
                    if obi + 3 < len(ob_order):
                        rec_wdma(obi + 3)
                    if c == 0:
                        if obi == 36: rec_xch(0, CH)
                        elif obi == 38: rec_xch(1, CH)
                        elif obi == 40: rec_xch(2, CH)
                        elif obi == 39: rec_sq(0)
                        elif obi == 41: rec_sq(1)
                        elif obi == 42: rec_mms(0, pre_ps[0], b_pre[0])
                        elif obi == 43: rec_mms(1, pre_ps[1], b_pre[1])
                        elif obi == 45: rec_chain(0, pre_ps[0], b_pre[0])
                        elif obi == 46: rec_chain(1, pre_ps[1], b_pre[1])
                while pend_rot:
                    pend_rot.pop(0)()
                sch.fence_psum(b_sqp[0], b_sqp[1], b_rotp, *( [b_pre[0], b_pre[1]] if c == 0 else []))

        if debug:
            dap = bass.AP(mid2_dump, 0, [[S, 64], [1, S]])
            sch.dma("sy", "dbgM", lambda s, o=dap, i=mid2_sb[0:64, :]:
                    s.dma_start(o, i), reads=[b_mid2], writes=[_Buf("dmpm")])
        sch.barrier()

    with ExitStack() as p23:
        OTr = p23.enter_context(nc.sbuf_tensor("OTr", [128, NQH * S], BF16))
        bcat_sb = p23.enter_context(nc.sbuf_tensor("bcat_sb", [64, 4096], BF16))
        wo = [p23.enter_context(nc.sbuf_tensor(f"wo{i}", [128, 4096], BF16)) for i in range(2)]
        b_OTr = _Buf("OTr")
        b_wo = [_Buf(f"wo{i}") for i in range(2)]
        sch.dma("gd", "c4", lambda g, o=bcat_sb[:], i=bc_in.ap(): g.dma_start(o, i), writes=[b_bcat])

        def rec_wo_dma(nb):
            sl = nb % 2
            woap = bass.AP(Wo_blk, nb * 128 * 4096, [[4096, 128], [1, 4096]])
            sch.dma("gd", f"wo{sl}", lambda g, o=wo[sl][:], i=woap:
                    g.dma_start(o, i), writes=[b_wo[sl]])

        rec_wo_dma(0); rec_wo_dma(1)

        with ExitStack() as p2:
            ktg = [p2.enter_context(nc.sbuf_tensor(f"ktg{i}", [128, S], BF16)) for i in range(2)]
            vg = [p2.enter_context(nc.sbuf_tensor(f"vg{i}", [128, S], BF16)) for i in range(2)]
            qt = [p2.enter_context(nc.sbuf_tensor(f"qt{i}", [128, TT], BF16)) for i in range(4)]
            pt_sb = [p2.enter_context(nc.sbuf_tensor(f"pt{i}", [128, 2 * TT], BF16)) for i in range(4)]
            recf = p2.enter_context(nc.sbuf_tensor("recf", [1, TT], F32))
            recb = [p2.enter_context(nc.sbuf_tensor(f"recb{i}", [1, TT], BF16)) for i in range(2)]
            rb_sb = p2.enter_context(nc.sbuf_tensor("rb_sb", [128, TT], F32))

            pt_ps = [p2.enter_context(nc.psum_tensor(f"ptp{i}", [128, 2 * TT], F32)) for i in range(2)]
            ot_ps = [p2.enter_context(nc.psum_tensor(f"otp{i}", [128, TT], F32)) for i in range(2)]
            denrb_ps = [p2.enter_context(nc.psum_tensor(f"denrb{i}", [128, TT], F32)) for i in range(2)]

            b_ktg = [_Buf(f"ktg{i}") for i in range(2)]
            b_vg = [_Buf(f"vg{i}") for i in range(2)]
            b_qt = [_Buf(f"qt{i}") for i in range(4)]
            b_ptsb = [_Buf(f"pt{i}") for i in range(4)]
            b_recf = _Buf("recf"); b_recb = [_Buf(f"recb{i}") for i in range(2)]; b_rbsb = _Buf("rb_sb")
            b_ptps = [sch.new_psum_buf(f"ptp{i}") for i in range(2)]
            b_otps = [sch.new_psum_buf(f"otp{i}") for i in range(2)]
            b_denrb = [sch.new_psum_buf(f"denrb{i}") for i in range(2)]

            def rec_kv_dma(g):
                sl = g % 2
                kin = bass.AP(KTs, g * 128 * S, [[S, 128], [1, S]])
                sch.dma("sy", f"ktg{sl}", lambda s, o=ktg[sl][:], i=kin: s.dma_start(o, i),
                        reads=[dbuf["KTs"]], writes=[b_ktg[sl]])
                vin = bass.AP(Vs, g * 128, [[KVO, 128], [128 * KVO, S // 128], [1, 128]])
                sch.dma("sy", f"vg{sl}", lambda s, o=vg[sl][:], i=vin: s.dma_start(o, i),
                        reads=[dbuf["Vs"]], writes=[b_vg[sl]])

            rec_kv_dma(0)
            pending_tail = []
            for g in range(NKV):
                gsl = g % 2
                if g + 1 < NKV:
                    rec_kv_dma(g + 1)
                iters = [(g * 4 + 2 * hp + st_, tile)
                         for hp in range(2) for tile in range(NTILE) for st_ in range(2)]

                def rec_qt(i):
                    hh, tt_ = iters[i]
                    qsl = i % 4
                    qin = bass.AP(QTs, hh * 128 * S + tt_ * TT, [[S, 128], [1, TT]])
                    sch.dma("sy", f"qt{qsl}", lambda s, o=qt[qsl][:], i=qin: s.dma_start(o, i),
                            reads=[dbuf["QTs"]], writes=[b_qt[qsl]])

                rec_qt(0); rec_qt(1)
                it = 0
                for hp in range(2):
                    for tile in range(NTILE):
                        heads = (g * 4 + 2 * hp, g * 4 + 2 * hp + 1)
                        qsl2 = (it % 4, (it + 1) % 4)
                        if it + 2 < len(iters):
                            rec_qt(it + 2)
                        if it + 3 < len(iters):
                            rec_qt(it + 3)
                        t0 = tile * TT
                        nsb = t0 // 128 + 4
                        npair = nsb // 2

                        def rec_scores(j, st_):
                            par = st_
                            for half in range(2):
                                sb = 2 * j + half
                                vis0 = max(0, sb * 128 - t0)
                                sch.op("pe", lambda t,
                                       o=pt_ps[par][:, half * TT + vis0: (half + 1) * TT],
                                       l=ktg[gsl][:, sb * 128:(sb + 1) * 128],
                                       r=qt[qsl2[st_]][:, vis0:TT]:
                                       t.matmul(o, l, r, start=True, stop=True),
                                       reads=[b_ktg[gsl], b_qt[qsl2[st_]]], writes=[b_ptps[par]])

                        def rec_exp(j, st_, slot):
                            par = st_
                            vis = [max(0, (2 * j + half) * 128 - t0) for half in range(2)]
                            if vis[0] == 0 and vis[1] == 0:
                                sch.op("sc", lambda s, o=pt_sb[slot][:, 0:2 * TT],
                                       i=pt_ps[par][:, 0:2 * TT]:
                                       s.activation(o, i, AF.Exp, bias=SHIFT_c, scale=ISD_c),
                                       reads=[b_ptps[par], b_cf], writes=[b_ptsb[slot]])
                            else:
                                for half in range(2):
                                    v0 = half * TT + vis[half]
                                    sch.op("sc", lambda s, o=pt_sb[slot][:, v0:(half + 1) * TT],
                                           i=pt_ps[par][:, v0:(half + 1) * TT]:
                                           s.activation(o, i, AF.Exp, bias=SHIFT_c, scale=ISD_c),
                                           reads=[b_ptps[par], b_cf], writes=[b_ptsb[slot]])
                            for half in range(2):
                                sb = 2 * j + half
                                if sb * 128 >= t0:
                                    v0 = half * TT + vis[half]
                                    sch.op("ve", lambda v, o=pt_sb[slot][:, v0:v0 + 128],
                                           a=pt_sb[slot][:, v0:v0 + 128], bb=mask01:
                                           v.tensor_tensor(o, a, bb, ALU.mult),
                                           reads=[b_ptsb[slot], b_const], writes=[b_ptsb[slot]])

                        def rec_pv(j, st_, slot):
                            vis = [max(0, (2 * j + half) * 128 - t0) for half in range(2)]
                            first = (j == 0)
                            last = (j == npair - 1)
                            for half in range(2):
                                sb = 2 * j + half
                                v0 = vis[half]
                                sch.op("pe", lambda t, o=ot_ps[st_][:, v0:TT],
                                       l=vg[gsl][:, sb * 128:(sb + 1) * 128],
                                       r=pt_sb[slot][:, half * TT + v0:(half + 1) * TT],
                                       st=(first and half == 0), sp=(last and half == 1):
                                       t.matmul(o, l, r, start=st, stop=sp),
                                       reads=[b_vg[gsl], b_ptsb[slot]], writes=[b_otps[st_]])
                            for half in range(2):
                                sb = 2 * j + half
                                v0 = vis[half]
                                sch.op("pe", lambda t, o=denrb_ps[st_][0:1, v0:TT],
                                       l=ones_col,
                                       r=pt_sb[slot][:, half * TT + v0:(half + 1) * TT],
                                       st=(first and half == 0), sp=(last and half == 1):
                                       t.matmul(o, l, r, start=st, stop=sp),
                                       reads=[b_const, b_ptsb[slot]], writes=[b_denrb[st_]])

                        def slot_of(j, st_):
                            return (2 * j + st_) % 4

                        for j in range(npair):
                            rec_scores(j, 0)
                            rec_scores(j, 1)
                            while pending_tail:
                                pending_tail.pop(0)()
                            rec_exp(j, 0, slot_of(j, 0))
                            rec_exp(j, 1, slot_of(j, 1))
                            if j >= 1:
                                rec_pv(j - 1, 0, slot_of(j - 1, 0))
                                rec_pv(j - 1, 1, slot_of(j - 1, 1))
                        rec_pv(npair - 1, 0, slot_of(npair - 1, 0))
                        rec_pv(npair - 1, 1, slot_of(npair - 1, 1))

                        def rec_tail(h_=heads, t0=t0):
                            for st_ in range(2):
                                rf = recf[0:1, 0:TT]
                                sch.op("sc", lambda s, o=rf, i=denrb_ps[st_][0:1, :], st_=st_:
                                       s.activation(o, i, AF.Ln), reads=[b_denrb[st_]], writes=[b_recf])
                                sch.op("sc", lambda s, o=recb[st_][0:1, :], i=rf, st_=st_:
                                       s.activation(o, i, AF.Exp, scale=NEG1_c),
                                       reads=[b_recf, b_cf], writes=[b_recb[st_]])
                                sch.op("pe", lambda t, o=denrb_ps[st_][:], l=ones_row,
                                       r=recb[st_][0:1, :], st_=st_:
                                       t.matmul(o, l, r, start=True, stop=True),
                                       reads=[b_const, b_recb[st_]], writes=[b_denrb[st_]])
                                sch.op("ve", lambda v, o=rb_sb[:], i=denrb_ps[st_][:], st_=st_:
                                       v.tensor_copy(o, i),
                                       reads=[b_denrb[st_]], writes=[b_rbsb])
                                sch.op("ve", lambda v, o=OTr[:, h_[st_] * S + t0: h_[st_] * S + t0 + TT],
                                       a=ot_ps[st_][:], bb=rb_sb[:], st_=st_:
                                       v.tensor_tensor(o, a, bb, ALU.mult),
                                       reads=[b_otps[st_], b_rbsb], writes=[b_OTr])
                        pending_tail.append(rec_tail)
                        it += 2
            while pending_tail:
                pending_tail.pop(0)()
            sch.fence_psum(*(b_ptps + b_otps + b_denrb))
            if debug:
                for h in range(NQH):
                    oap = bass.AP(OT_dump, h * 128 * S, [[S, 128], [1, S]])
                    sch.dma("sy", "dbgO", lambda s, o=oap, i=OTr[:, h * S:(h + 1) * S]:
                            s.dma_start(o, i), reads=[b_OTr], writes=[_Buf("dump")])

        sch.barrier()

        with ExitStack() as p3:
            xt3 = [p3.enter_context(nc.sbuf_tensor(f"xt3{i}", [128, S], F32)) for i in range(2)]
            gt3 = [p3.enter_context(nc.sbuf_tensor(f"gt3{i}", [128, CH], BF16)) for i in range(2)]
            outst = [p3.enter_context(nc.sbuf_tensor(f"outst{i}", [128, CH], F32)) for i in range(2)]
            st_ps = [p3.enter_context(nc.psum_tensor(f"stp{i}", [128, CH], F32)) for i in range(2)]
            dl_ps = [p3.enter_context(nc.psum_tensor(f"dlp{i}", [128, CH], F32)) for i in range(2)]
            b_xt3 = [_Buf(f"xt3{i}") for i in range(2)]
            b_gt3 = [_Buf(f"gt3{i}") for i in range(2)]
            b_outst = [_Buf(f"outst{i}") for i in range(2)]
            b_stps = [sch.new_psum_buf(f"stp{i}") for i in range(2)]
            b_dlps = [sch.new_psum_buf(f"dlp{i}") for i in range(2)]

            def rec_p3_dma(nb, skip_wo=False):
                if not skip_wo:
                    rec_wo_dma(nb)
                sl = nb % 2
                xin = bass.AP(xT, nb * 128 * S, [[S, 128], [1, S]])
                sch.dma("sy", f"xt3{sl}", lambda s, o=xt3[sl][:], i=xin: s.dma_start(o, i),
                        reads=[dbuf["xT"]], writes=[b_xt3[sl]])

            rec_p3_dma(0, skip_wo=True); rec_p3_dma(1, skip_wo=True)
            it3 = 0
            for nb in range(32):
                sl = nb % 2
                for tc in range(S // CH):
                    ps = it3 % 2
                    it3 += 1
                    for ob in range(32):
                        for ck in range(CH // 512):
                            sch.op("pe", lambda t, o=st_ps[ps][:, ck * 512:(ck + 1) * 512],
                                   l=wo[sl][:, ob * 128:(ob + 1) * 128],
                                   r=OTr[:, ob * S + tc * CH + ck * 512: ob * S + tc * CH + (ck + 1) * 512],
                                   st=(ob == 0), sp=(ob == 31):
                                   t.matmul(o, l, r, start=st, stop=sp),
                                   reads=[b_wo[sl], b_OTr], writes=[b_stps[ps]],
                                   mark=(ob == 31))
                    for ck in range(CH // 512):
                        sch.op("pe", lambda t, o=dl_ps[ps][:, ck * 512:(ck + 1) * 512],
                               l=bcat_sb[0:64, nb * 128:(nb + 1) * 128],
                               r=mid2_sb[0:64, tc * CH + ck * 512: tc * CH + (ck + 1) * 512]:
                               t.matmul(o, l, r, start=True, stop=True),
                               reads=[b_bcat, b_mid2], writes=[b_dlps[ps]],
                               mark=(ck == CH // 512 - 1))
                    sch.op("sc", lambda s, o=gt3[ps][:], i=dl_ps[ps][:]:
                           s.activation(o, i, AF.Silu),
                           reads=[b_dlps[ps]], writes=[b_gt3[ps]])
                    sch.op("ve", lambda v, o=outst[ps][:], a=st_ps[ps][:], bb=gt3[ps][:]:
                           v.tensor_tensor(o, a, bb, ALU.mult),
                           reads=[b_stps[ps], b_gt3[ps]], writes=[b_outst[ps]])
                    sch.op("ve", lambda v, o=outst[ps][:], a=outst[ps][:],
                           bb=xt3[sl][:, tc * CH:(tc + 1) * CH]:
                           v.tensor_tensor(o, a, bb, ALU.add),
                           reads=[b_outst[ps], b_xt3[sl]], writes=[b_outst[ps]])
                    oap = bass.AP(outT, nb * 128 * S + tc * CH, [[S, 128], [1, CH]])
                    sch.dma("sy", f"so{ps}", lambda s, o=oap, i=outst[ps][:]:
                            s.dma_start(o, i), reads=[b_outst[ps]], writes=[dbuf["outT"]])
                if nb + 2 < 32:
                    rec_p3_dma(nb + 2)
            sch.fence_psum(*(b_stps + b_dlps))

    sch.barrier()

    with nc.Block() as block:
        semstack = sch.emit(nc, block)
    semstack.close()
    st.close()
    return nc



def _host_shared(Wq, Wk, Wv, Wo, A, Bm, Wr):
    f32 = np.float32
    WcatT = np.concatenate([Wq.T, Wk.T, Wv.T], axis=1).astype(f32)
    Wblk = np.ascontiguousarray(
        WcatT.reshape(32, 128, 48, 128).transpose(2, 1, 0, 3).reshape(48, 128, 4096))
    WoT = Wo.T.astype(f32)
    Wo_blk = np.ascontiguousarray(
        WoT.reshape(32, 128, 32, 128).transpose(2, 1, 0, 3).reshape(32, 128, 4096))
    wr_blk = np.ascontiguousarray(
        Wr.T.astype(f32).reshape(32, 128, 4).transpose(1, 0, 2).reshape(128, 128))
    a_blk = np.ascontiguousarray(
        A.transpose(2, 0, 1).reshape(4096, 64).astype(f32)
        .reshape(32, 128, 64).transpose(1, 0, 2).reshape(128, 2048))
    bcat = np.ascontiguousarray(Bm.transpose(0, 2, 1).reshape(64, 4096).astype(f32))

    consts = np.zeros((128, 576), f32)
    consts[:, 0:128] = np.eye(128, dtype=f32)
    prot = np.zeros((128, 128), f32)
    for d in range(64):
        prot[d + 64, d] = -1.0
        prot[d, d + 64] = 1.0
    consts[:, 128:256] = prot
    m01 = np.zeros((128, 128), f32)
    for s_ in range(128):
        m01[s_, s_:] = 1.0
    consts[:, 256:384] = m01
    e4 = np.zeros((128, 64), f32)
    for e in range(4):
        e4[e, e * 16:(e + 1) * 16] = 4.0
    consts[:, 384:448] = e4
    consts[:, 448:576] = 1.0
    cf32 = np.zeros((128, 8), f32)
    cf32[:, 0] = 1e-5
    cf32[:, 1] = -0.5
    cf32[:, 2] = 1.0 / math.sqrt(128.0)
    cf32[:, 3] = -10.0
    cf32[:, 4] = 1.0 / 4096.0
    cf32[:, 5] = -1.0
    cf32[:, 6] = -1e9

    freqs = np.exp(-np.arange(0, 128, 2, dtype=f32) * (math.log(10000.0) / 128)).astype(f32)
    ang = np.arange(S, dtype=f32)[:, None] * freqs[None, :]
    cosT = np.cos(ang).T.astype(f32)
    sinT = np.sin(ang).T.astype(f32)
    trig = np.zeros((128, 4096), f32)
    trig[0:64, 0:S] = cosT
    trig[64:128, 0:S] = cosT
    trig[0:64, 2048:2048 + S] = sinT
    trig[64:128, 2048:2048 + S] = sinT
    return dict(Wblk=Wblk, Wo_blk=Wo_blk, wr_blk=wr_blk, a_blk=a_blk,
                bcat=bcat, consts=consts, cf32=cf32, trig=trig)


def kernel(L0_attn_hidden_states, L0_attn_Wq, L0_attn_Wk, L0_attn_Wv, L0_attn_Wo,
           L0_ffn_A, L0_ffn_B, L0_ffn_Wr):
    x = np.asarray(L0_attn_hidden_states, dtype=np.float32)
    shared = _host_shared(np.asarray(L0_attn_Wq), np.asarray(L0_attn_Wk),
                          np.asarray(L0_attn_Wv), np.asarray(L0_attn_Wo),
                          np.asarray(L0_ffn_A), np.asarray(L0_ffn_B),
                          np.asarray(L0_ffn_Wr))
    nc = build_nc()
    in_maps = []
    for i in range(8):
        m = {"xT": np.ascontiguousarray(x[i].T)}
        m.update(shared)
        in_maps.append(m)
    res = run_bass_kernel_spmd(nc, in_maps, core_ids=list(range(8)))
    out = np.empty((8, S, H), dtype=np.float32)
    for i in range(8):
        out[i] = res.results[i]["outT"].T
    return out


# version 21
# speedup vs baseline: 1.2011x; 1.0081x over previous
import math
from contextlib import ExitStack

import numpy as np

import concourse.bass as bass
import concourse.mybir as mybir
from concourse.bass_utils import run_bass_kernel_spmd

F32 = mybir.dt.float32
BF16 = mybir.dt.bfloat16
AF = mybir.ActivationFunctionType
ALU = mybir.AluOpType
AX = mybir.AxisListType

S = 2048
H = 4096
NHB = H // 128
NQH = 32
NKV = 8
D = 128
CH = 1024
NCH = S // CH
SUB = 128
NSUB = CH // SUB
KVO = NKV * D
NOB = NQH + NKV + NKV
NGRP = NOB // 2
TT = 512
NTILE = S // TT
INV_SQRT_D = 1.0 / math.sqrt(D)
EXP_SHIFT = -10.0

ENG_COMPUTE = ("pe", "sc", "ve")
ENG_ALL = ("pe", "sc", "ve", "sy", "gd")


class _Buf:
    __slots__ = ("name", "last_w", "readers", "no_waw")

    def __init__(self, name, no_waw=False):
        self.name = name
        self.last_w = None
        self.readers = {}
        self.no_waw = no_waw


class _Sched:

    def __init__(self):
        self.q = {e: [] for e in ENG_ALL}
        self.cum = {e: 0 for e in ENG_COMPUTE}
        self.streams = {}
        self.satisfied = {e: {} for e in ENG_ALL}
        self.psum_fence = []
        self.nops = 0

    def _record(self, eng, fn, reads, writes, stream, mark):
        waits = {}
        sat = self.satisfied[eng]

        def need(tok, what):
            if tok is None:
                return
            sem, val = tok
            if sem == "eng:pe" and eng == "pe":
                return
            if val is None:
                raise AssertionError(f"dep on unmarked op ({what}) eng={eng}")
            if sat.get(sem, 0) >= val:
                return
            waits[sem] = max(waits.get(sem, 0), val)

        for b in reads:
            need(b.last_w, f"RAW {b.name}")
        for b in writes:
            if not (b.no_waw and stream is not None):
                need(b.last_w, f"WAW {b.name}")
            for tok in b.readers.values():
                need(tok, f"WAR {b.name}")

        for sem, val in waits.items():
            sat[sem] = max(sat.get(sem, 0), val)

        if stream is not None:
            tot = self.streams.get(stream, 0) + 16
            self.streams[stream] = tot
            tok = ("st:" + stream, tot)
            inc = ("st:" + stream, 16)
        elif mark:
            self.cum[eng] += 1
            tok = ("eng:" + eng, self.cum[eng])
            inc = ("eng:" + eng, 1)
        else:
            tok = ("eng:" + eng, None)
            inc = None

        for b in reads:
            b.readers[eng] = tok
        for b in writes:
            b.last_w = tok
            b.readers = {}

        self.q[eng].append((waits, fn, inc))
        return tok

    def op(self, eng, fn, reads=(), writes=(), mark=True):
        return self._record(eng, fn, reads, writes, None, mark)

    def dma(self, eng, stream, fn, reads=(), writes=()):
        return self._record(eng, fn, reads, writes, stream, True)

    def barrier(self):
        targets = {}
        for e in ENG_COMPUTE:
            if self.cum[e]:
                targets["eng:" + e] = self.cum[e]
        for st, tot in self.streams.items():
            targets["st:" + st] = tot
        for e in ENG_ALL:
            sat = self.satisfied[e]
            waits = {}
            for sem, val in targets.items():
                if e == "pe" and sem == "eng:pe":
                    pass
                if sat.get(sem, 0) >= val:
                    continue
                waits[sem] = val
                sat[sem] = val
            if waits:
                self.nops += 1
                self.q[e].append((waits, lambda eng: eng.nop(), None))

    def fence_psum(self, *bufs):
        for b in bufs:
            if b.last_w is not None:
                self.psum_fence.append(b.last_w)
            self.psum_fence.extend(b.readers.values())

    def new_psum_buf(self, name):
        b = _Buf(name)
        for i, tok in enumerate(self.psum_fence):
            b.readers["_f%d" % i] = tok
        return b

    def emit(self, nc, block):
        sems = {}
        stack = ExitStack()
        for e in ENG_COMPUTE:
            sems["eng:" + e] = stack.enter_context(nc.semaphore("s_" + e))
        for st in self.streams:
            sems["st:" + st] = stack.enter_context(nc.semaphore("s_" + st))
        engmap = {"pe": block.tensor, "sc": block.scalar, "ve": block.vector,
                  "sy": block.sync, "gd": block.gpsimd}

        def run(e):
            def body(eng):
                for waits, fn, inc in self.q[e]:
                    for sem, val in sorted(waits.items()):
                        eng.wait_ge(sems[sem], val)
                    inst = fn(eng)
                    if inc is not None:
                        inst.then_inc(sems[inc[0]], inc[1])
            engmap[e](body)

        for e in ENG_ALL:
            run(e)
        return stack


def build_nc(debug=False):
    nc = bass.Bass("TRN2", target_bir_lowering=False)
    sch = _Sched()

    xT = nc.dram_tensor("xT", [H, S], F32, kind="ExternalInput")
    Wblk = nc.dram_tensor("Wblk", [NOB, 128, 4096], F32, kind="ExternalInput")
    Wo_blk = nc.dram_tensor("Wo_blk", [32, 128, 4096], F32, kind="ExternalInput")
    wr_in = nc.dram_tensor("wr_blk", [128, 128], F32, kind="ExternalInput")
    a_in = nc.dram_tensor("a_blk", [128, 2048], F32, kind="ExternalInput")
    bc_in = nc.dram_tensor("bcat", [64, 4096], F32, kind="ExternalInput")
    cn_in = nc.dram_tensor("consts", [128, 576], F32, kind="ExternalInput")
    cf_in = nc.dram_tensor("cf32", [128, 8], F32, kind="ExternalInput")
    tg_in = nc.dram_tensor("trig", [128, 4096], F32, kind="ExternalInput")
    outT = nc.dram_tensor("outT", [H, S], F32, kind="ExternalOutput")
    skind = "ExternalOutput" if debug else "Internal"
    QTs = nc.dram_tensor("QTs", [NQH * D, S], BF16, kind=skind)
    KTs = nc.dram_tensor("KTs", [KVO, S], BF16, kind=skind)
    Vs = nc.dram_tensor("Vs", [S, KVO], BF16, kind=skind)
    if debug:
        xnT_dump = nc.dram_tensor("xnT_dump", [H, S], BF16, kind="ExternalOutput")
        mid2_dump = nc.dram_tensor("mid2_dump", [64, S], BF16, kind="ExternalOutput")
        OT_dump = nc.dram_tensor("OT_dump", [NQH * D, S], BF16, kind="ExternalOutput")

    dbuf = {}
    for t in ("xT", "QTs", "KTs", "Vs", "outT"):
        dbuf[t] = _Buf(t, no_waw=True)

    st = ExitStack()

    const_sb = st.enter_context(nc.sbuf_tensor("const_sb", [128, 576], BF16))
    cf_sb = st.enter_context(nc.sbuf_tensor("cf_sb", [128, 8], F32))
    trig_sb = st.enter_context(nc.sbuf_tensor("trig_sb", [128, 4096], BF16))
    wr_sb = st.enter_context(nc.sbuf_tensor("wr_sb", [128, 128], BF16))
    at_sb = st.enter_context(nc.sbuf_tensor("at_sb", [128, 2048], BF16))
    mid2_sb = st.enter_context(nc.sbuf_tensor("mid2_sb", [64, S], BF16))

    b_const = _Buf("const"); b_trig = _Buf("trig"); b_wr = _Buf("wr")
    b_at = _Buf("at"); b_bcat = _Buf("bcat"); b_mid2 = _Buf("mid2")

    EPS_c = cf_sb[0:1, 0:1]
    NHALF_c = cf_sb[0:1, 1:2]
    ISD_c = cf_sb[:, 2:3]
    SHIFT_c = cf_sb[:, 3:4]
    INVH_c = cf_sb[0:1, 4:5]
    NEG1_c = cf_sb[0:1, 5:6]
    NEGBIG_c = cf_sb[:, 6:7]

    ident = const_sb[:, 0:128]
    prot = const_sb[:, 128:256]
    mask01 = const_sb[:, 256:384]
    e4 = const_sb[0:4, 384:448]
    ones_col = const_sb[:, 448:449]
    ones_row = const_sb[0:1, 448:576]

    sch.dma("gd", "c0", lambda g, o=const_sb[:], i=cn_in.ap(): g.dma_start(o, i), writes=[b_const])
    sch.dma("gd", "c1", lambda g, o=trig_sb[:], i=tg_in.ap(): g.dma_start(o, i), writes=[b_trig])
    sch.dma("gd", "c2", lambda g, o=wr_sb[:], i=wr_in.ap(): g.dma_start(o, i), writes=[b_wr])
    sch.dma("gd", "c3", lambda g, o=at_sb[:], i=a_in.ap(): g.dma_start(o, i), writes=[b_at])
    b_cf = _Buf("cf")
    sch.dma("sy", "c5", lambda s, o=cf_sb[:], i=cf_in.ap(): s.dma_start(o, i), writes=[b_cf])

    with ExitStack() as p1:
        xnT = p1.enter_context(nc.sbuf_tensor("xnT", [128, NHB * CH], BF16))
        xsq = [p1.enter_context(nc.sbuf_tensor(f"xsq{i}", [128, 2 * NHB * SUB], BF16)) for i in range(3)]
        wg = [p1.enter_context(nc.sbuf_tensor(f"wg{i}", [128, 4096], BF16)) for i in range(3)]
        stat = [p1.enter_context(nc.sbuf_tensor("stat0", [1, 6 * SUB], F32))] * 2
        statb = [p1.enter_context(nc.sbuf_tensor("statb0", [1, 2 * SUB], BF16))] * 2
        rbnb = [p1.enter_context(nc.sbuf_tensor(f"rbnb{i}", [128, 2 * SUB], BF16)) for i in range(2)]
        qraw = [p1.enter_context(nc.sbuf_tensor(f"qraw{i}", [128, CH], BF16)) for i in range(2)]
        rotb = [p1.enter_context(nc.sbuf_tensor(f"rotb{i}", [128, CH], BF16)) for i in range(2)]
        qsp = [p1.enter_context(nc.sbuf_tensor(f"qsp{i}", [128, CH], BF16)) for i in range(2)]
        vstag = p1.enter_context(nc.sbuf_tensor("vstag", [128, CH], BF16))
        lgb = p1.enter_context(nc.sbuf_tensor("lgb", [4, CH], BF16))
        lg_nat = p1.enter_context(nc.sbuf_tensor("lg_nat", [128, (CH // 128) * 4], F32))
        gt_sc = p1.enter_context(nc.sbuf_tensor("gt_sc", [128, 112], F32))
        gt_bfn = p1.enter_context(nc.sbuf_tensor("gt_bfn", [128, (CH // 128) * 4], BF16))
        gatesT = p1.enter_context(nc.sbuf_tensor("gatesT", [4, CH], BF16))
        g4_bf = p1.enter_context(nc.sbuf_tensor("g4_bf", [64, CH], BF16))

        b_xnT = [_Buf("xnT_lo"), _Buf("xnT_hi")]
        b_xa = [_Buf(f"xa{i}") for i in range(3)]
        b_xs = [_Buf(f"xs{i}") for i in range(3)]
        b_wg = [_Buf(f"wg{i}") for i in range(3)]
        b_stat = [_Buf("stat0")] * 2
        b_statb = [_Buf("statb0")] * 2
        b_rbnb = [_Buf(f"rbnb{i}") for i in range(2)]
        b_qraw = [_Buf(f"qraw{i}") for i in range(2)]
        b_rotb = [_Buf(f"rotb{i}") for i in range(2)]
        b_qsp = [_Buf(f"qsp{i}") for i in range(2)]
        b_vstag = _Buf("vstag")
        b_lgb = _Buf("lgb"); b_lgnat = _Buf("lg_nat"); b_gtsc = _Buf("gt_sc")
        b_gtbfn = _Buf("gt_bfn"); b_gatesT = _Buf("gatesT"); b_g4 = _Buf("g4_bf")

        NS = NHB * SUB

        def rec_xch(sub, c0):
            sl_ = sub % 3
            t0c = c0 + sub * SUB
            in_ap = bass.AP(xT, t0c, [[S, 128], [128 * S, NHB], [1, SUB]])
            sch.dma("gd", f"xch{sl_}",
                    lambda g, o=xsq[sl_][:, 0:NS], i=in_ap: g.dma_start(o, i),
                    reads=[dbuf["xT"]], writes=[b_xa[sl_]])

        def rec_sq(sub):
            sl = sub % 3
            sch.op("sc", lambda s, o=xsq[sl][:, NS:2 * NS], i=xsq[sl][:, 0:NS]:
                   s.activation(o, i, AF.Square),
                   reads=[b_xa[sl]], writes=[b_xs[sl]])

        def rec_mms(sub, prp_t, b_prp_t):
            sl = sub % 3
            for hb in range(NHB):
                rhs = bass.AP(xsq[sl], hb * SUB, [[2 * NS, 128], [NS, 2], [1, SUB]])
                sch.op("pe", lambda t, o=prp_t[0:1, 0:2 * SUB], l=ones_col, r=rhs,
                       st=(hb == 0), sp=(hb == NHB - 1):
                       t.matmul(o, l, r, start=st, stop=sp),
                       reads=[b_const, b_xa[sl], b_xs[sl]], writes=[b_prp_t])

        def rec_chain(sub, prp_t, b_prp_t):
            sl = sub % 3
            stp, b_stp = stat[sl % 2], b_stat[sl % 2]
            sbp, b_sbp = statb[sl % 2], b_statb[sl % 2]
            m = stp[0:1, 0:SUB]; ex2 = stp[0:1, SUB:2 * SUB]
            var = stp[0:1, 2 * SUB:3 * SUB]; lnv = stp[0:1, 5 * SUB:6 * SUB]
            sch.op("ve", lambda v, o=stp[0:1, 0:2 * SUB], i=prp_t[0:1, 0:2 * SUB]:
                   v.tensor_scalar(o, i, INVH_c, None, ALU.mult),
                   reads=[b_prp_t, b_cf], writes=[b_stp])
            sch.op("ve", lambda v, o=var, a=m, bb=m:
                   v.tensor_tensor(o, a, bb, ALU.mult), reads=[b_stp], writes=[b_stp])
            sch.op("ve", lambda v, o=var, a=ex2, bb=var:
                   v.tensor_tensor(o, a, bb, ALU.subtract), reads=[b_stp], writes=[b_stp])
            sch.op("sc", lambda s, o=lnv, i=var: s.activation(o, i, AF.Ln, bias=EPS_c),
                   reads=[b_stp, b_cf], writes=[b_stp])
            sch.op("sc", lambda s, o=sbp[0:1, 0:SUB], i=lnv:
                   s.activation(o, i, AF.Exp, scale=NHALF_c),
                   reads=[b_stp, b_cf], writes=[b_sbp])
            sch.op("ve", lambda v, o=sbp[0:1, SUB:2 * SUB], i0=m, i1=sbp[0:1, 0:SUB]:
                   v.scalar_tensor_tensor(o, i0, NEG1_c, i1, ALU.mult, ALU.mult),
                   reads=[b_stp, b_sbp, b_cf], writes=[b_sbp])
            sch.op("pe", lambda t, o=prp_t[:], l=ones_row, r=sbp[0:1, 0:2 * SUB]:
                   t.matmul(o, l, r, start=True, stop=True),
                   reads=[b_const, b_sbp], writes=[b_prp_t])
            sch.op("sc", lambda s, o=rbnb[sl % 2][:], i=prp_t[:]: s.copy(o, i),
                   reads=[b_prp_t], writes=[b_rbnb[sl % 2]])

        def rec_stats(sub, c0, prp_t, b_prp_t):
            rec_sq(sub)
            rec_mms(sub, prp_t, b_prp_t)
            rec_chain(sub, prp_t, b_prp_t)

        for c in range(NCH):
            c0 = c * CH

            with ExitStack() as pa:
                prp = [pa.enter_context(nc.psum_tensor(f"prp{c}_{i}", [128, 2 * SUB], F32)) for i in range(2)]
                b_prp = [sch.new_psum_buf(f"prp{i}") for i in range(2)]

                if c == 0:
                    rec_xch(0, c0); rec_xch(1, c0); rec_xch(2, c0)
                for sub in range(NSUB):
                    sl = sub % 3
                    if not (c == 1 and sub < 2):
                        rec_stats(sub, c0, prp[sub % 2], b_prp[sub % 2])
                    rb_bc = bass.AP(rbnb[sl % 2], 0, [[2 * SUB, 128], [0, 4], [1, SUB]])
                    nb_bc = bass.AP(rbnb[sl % 2], SUB, [[2 * SUB, 128], [0, 4], [1, SUB]])
                    for hb in range(0, NHB, 4):
                        xb4 = bass.AP(xsq[sl], hb * SUB, [[2 * NS, 128], [SUB, 4], [1, SUB]])
                        dst4 = bass.AP(xnT, hb * CH + sub * SUB, [[NHB * CH, 128], [CH, 4], [1, SUB]])
                        t4 = bass.AP(xsq[sl], NS + hb * SUB, [[2 * NS, 128], [SUB, 4], [1, SUB]])
                        sch.op("ve", lambda v, o=t4, a=xb4, bb=rb_bc:
                               v.tensor_tensor(o, a, bb, ALU.mult),
                               reads=[b_xa[sl], b_rbnb[sl % 2]], writes=[b_xs[sl]])
                        sch.op("ve", lambda v, o=dst4, a=t4, bb=nb_bc:
                               v.tensor_tensor(o, a, bb, ALU.add),
                               reads=[b_xs[sl], b_rbnb[sl % 2]],
                               writes=[b_xnT[sub * SUB // 512]])
                    if sub + 3 < NSUB:
                        rec_xch(sub + 3, c0)
                sch.fence_psum(b_prp[0], b_prp[1])
                if debug:
                    for hb in range(NHB):
                        dap = bass.AP(xnT_dump, hb * 128 * S + c0, [[S, 128], [1, CH]])
                        sch.dma("sy", "dbgX", lambda s, o=dap, i=xnT[:, hb * CH:(hb + 1) * CH]:
                                s.dma_start(o, i), reads=list(b_xnT), writes=[_Buf("dmpx")])

            ob_order = list(range(40, 48)) + list(range(32, 40)) + list(range(32))

            def rec_wdma(obi):
                ob = ob_order[obi]
                sl = obi % 3
                wap = bass.AP(Wblk, ob * 128 * 4096, [[4096, 128], [1, 4096]])
                sch.dma("gd", f"wg{sl}",
                        lambda g, o=wg[sl][:], i=wap: g.dma_start(o, i),
                        writes=[b_wg[sl]])

            with ExitStack() as pv:
                svp_l = [pv.enter_context(nc.psum_tensor(f"svp{c}_{i}", [128, CH], F32)) for i in range(2)]
                tp = [pv.enter_context(nc.psum_tensor(f"tp{c}_{i}", [128, 128], BF16)) for i in range(2)]
                b_svp_l = [sch.new_psum_buf(f"svp{i}") for i in range(2)]
                b_tp = [sch.new_psum_buf(f"tp{i}") for i in range(2)]
                rec_wdma(0); rec_wdma(1); rec_wdma(2)
                for obi in range(8):
                    ob = ob_order[obi]; sl = obi % 3
                    gv = ob - 40
                    svp, b_svp = svp_l[gv % 2], b_svp_l[gv % 2]
                    for ck in range(CH // 512):
                        for hb in range(NHB):
                            sch.op("pe", lambda t, o=svp[:, ck * 512:(ck + 1) * 512],
                                   l=wg[sl][:, hb * 128:(hb + 1) * 128],
                                   r=xnT[:, hb * CH + ck * 512: hb * CH + (ck + 1) * 512],
                                   st=(hb == 0), sp=(hb == NHB - 1):
                                   t.matmul(o, l, r, start=st, stop=sp),
                                   reads=[b_wg[sl], b_xnT[ck]], writes=[b_svp],
                                   mark=(hb == NHB - 1))
                    sch.op("sc", lambda s, o=qraw[0][:], i=svp[:]: s.copy(o, i),
                           reads=[b_svp], writes=[b_qraw[0]])
                    for sb in range(CH // 128):
                        sch.op("pe", lambda t, o=tp[sb % 2][:],
                               i=qraw[0][:, sb * 128:(sb + 1) * 128], idn=ident:
                               t.transpose(o, i, idn),
                               reads=[b_qraw[0], b_const], writes=[b_tp[sb % 2]])
                        sch.op("ve", lambda v, o=vstag[:, sb * 128:(sb + 1) * 128],
                               i=tp[sb % 2][:]: v.tensor_copy(o, i),
                               reads=[b_tp[sb % 2]], writes=[b_vstag])
                    out_ap = bass.AP(Vs, c0 * KVO + gv * 128,
                                     [[KVO, 128], [128 * KVO, CH // 128], [1, 128]])
                    sch.dma("sy", "spV",
                            lambda s, o=out_ap, i=vstag[:]: s.dma_start(o, i),
                            reads=[b_vstag], writes=[dbuf["Vs"]])
                    if obi + 3 < len(ob_order):
                        rec_wdma(obi + 3)
                sch.fence_psum(b_svp_l[0], b_svp_l[1], b_tp[0], b_tp[1])

            with ExitStack() as pb:
                lgp = pb.enter_context(nc.psum_tensor(f"lgp{c}", [4, CH], F32))
                ltp = [pb.enter_context(nc.psum_tensor(f"ltp{c}_{i}", [128, 4], BF16)) for i in range(2)]
                gtp = [pb.enter_context(nc.psum_tensor(f"gtp{c}_{i}", [4, 128], BF16)) for i in range(2)]
                b_lgp = sch.new_psum_buf("lgp")
                b_ltp = [sch.new_psum_buf(f"ltp{i}") for i in range(2)]
                b_gtp = [sch.new_psum_buf(f"gtp{i}") for i in range(2)]

                for hb in range(NHB):
                    for ck in range(CH // 512):
                        sch.op("pe", lambda t, o=lgp[0:4, ck * 512:(ck + 1) * 512],
                               l=wr_sb[:, hb * 4:(hb + 1) * 4],
                               r=xnT[:, hb * CH + ck * 512: hb * CH + (ck + 1) * 512],
                               st=(hb == 0), sp=(hb == NHB - 1):
                               t.matmul(o, l, r, start=st, stop=sp),
                               reads=[b_wr, b_xnT[ck]], writes=[b_lgp],
                               mark=(hb == NHB - 1))
                sch.op("sc", lambda s, o=lgb[:], i=lgp[0:4, :]: s.copy(o, i),
                       reads=[b_lgp], writes=[b_lgb])
                ntb = CH // 128
                for tb in range(ntb):
                    sch.op("pe", lambda t, o=ltp[tb % 2][:], i=lgb[0:4, tb * 128:(tb + 1) * 128],
                           idn=ident[0:4, 0:4]: t.transpose(o, i, idn),
                           reads=[b_lgb, b_const], writes=[b_ltp[tb % 2]])
                    sch.op("ve", lambda v, o=lg_nat[:, tb * 4:(tb + 1) * 4], i=ltp[tb % 2][:]:
                           v.tensor_copy(o, i), reads=[b_ltp[tb % 2]], writes=[b_lgnat])
                G4 = ntb * 4
                GSC = 3 * G4 + 2 * ntb

                def a3(t_, off, stride):
                    return bass.AP(t_, off, [[stride, 128], [4, ntb], [1, 4]])

                def abc(t_, off, stride):
                    return bass.AP(t_, off, [[stride, 128], [1, ntb], [0, 4]])

                dd = gt_sc[:, 0:G4]; e1 = gt_sc[:, G4:2 * G4]; msk = gt_sc[:, 2 * G4:3 * G4]
                m1 = gt_sc[:, 3 * G4:3 * G4 + ntb]; t2 = gt_sc[:, 3 * G4 + ntb:GSC]
                dd3 = a3(gt_sc, 0, GSC); msk3 = a3(gt_sc, 2 * G4, GSC)
                m1bc = abc(gt_sc, 3 * G4, GSC); t2bc = abc(gt_sc, 3 * G4 + ntb, GSC)
                sch.op("ve", lambda v, o=m1, i=a3(lg_nat, 0, G4): v.reduce_max(o, i, AX.X),
                       reads=[b_lgnat], writes=[b_gtsc])
                sch.op("ve", lambda v, o=dd3, a=a3(lg_nat, 0, G4), bb=m1bc:
                       v.tensor_tensor(o, a, bb, ALU.subtract),
                       reads=[b_lgnat, b_gtsc], writes=[b_gtsc])
                sch.op("ve", lambda v, o=e1, i=dd:
                       v.tensor_single_scalar(o, i, 0.0, ALU.is_equal),
                       reads=[b_gtsc], writes=[b_gtsc])
                sch.op("ve", lambda v, o=msk, i0=e1, i1=dd:
                       v.scalar_tensor_tensor(o, i0, NEGBIG_c, i1, ALU.mult, ALU.add),
                       reads=[b_gtsc, b_cf], writes=[b_gtsc])
                sch.op("ve", lambda v, o=t2, i=msk3: v.reduce_max(o, i, AX.X),
                       reads=[b_gtsc], writes=[b_gtsc])
                sch.op("ve", lambda v, o=msk3, a=dd3, bb=t2bc:
                       v.tensor_tensor(o, a, bb, ALU.is_ge),
                       reads=[b_gtsc], writes=[b_gtsc])
                sch.op("sc", lambda s, o=e1, i=dd: s.activation(o, i, AF.Exp),
                       reads=[b_gtsc], writes=[b_gtsc])
                sch.op("ve", lambda v, o=dd, a=e1, bb=msk:
                       v.tensor_tensor(o, a, bb, ALU.mult),
                       reads=[b_gtsc], writes=[b_gtsc])
                sch.op("ve", lambda v, o=m1, i=dd3: v.reduce_sum(o, i, AX.X),
                       reads=[b_gtsc], writes=[b_gtsc])
                sch.op("ve", lambda v, o=t2, i=m1: v.reciprocal(o, i),
                       reads=[b_gtsc], writes=[b_gtsc])
                sch.op("ve", lambda v, o=a3(gt_bfn, 0, G4), a=dd3, bb=t2bc:
                       v.tensor_tensor(o, a, bb, ALU.mult),
                       reads=[b_gtsc], writes=[b_gtbfn])
                for tb in range(ntb):
                    sch.op("pe", lambda t, o=gtp[tb % 2][:], i=gt_bfn[:, tb * 4:(tb + 1) * 4],
                           idn=ident: t.transpose(o, i, idn),
                           reads=[b_gtbfn, b_const], writes=[b_gtp[tb % 2]])
                    sch.op("sc", lambda s, o=gatesT[0:4, tb * 128:(tb + 1) * 128], i=gtp[tb % 2][:]:
                           s.copy(o, i), reads=[b_gtp[tb % 2]], writes=[b_gatesT])
                sch.fence_psum(b_lgp, b_ltp[0], b_ltp[1], b_gtp[0], b_gtp[1])

            with ExitStack() as pb2:
                g4p = pb2.enter_context(nc.psum_tensor(f"g4p{c}", [64, CH], F32))
                midp = pb2.enter_context(nc.psum_tensor(f"midp{c}", [64, CH], F32))
                b_g4p = sch.new_psum_buf("g4p")
                b_midp = sch.new_psum_buf("midp")
                for ck in range(CH // 512):
                    sch.op("pe", lambda t, o=g4p[0:64, ck * 512:(ck + 1) * 512],
                           l=e4, r=gatesT[0:4, ck * 512:(ck + 1) * 512]:
                           t.matmul(o, l, r, start=True, stop=True),
                           reads=[b_const, b_gatesT], writes=[b_g4p])
                sch.op("sc", lambda s, o=g4_bf[:], i=g4p[0:64, :]: s.copy(o, i),
                       reads=[b_g4p], writes=[b_g4])
                for hb in range(NHB):
                    for ck in range(CH // 512):
                        sch.op("pe", lambda t, o=midp[0:64, ck * 512:(ck + 1) * 512],
                               l=at_sb[:, hb * 64:(hb + 1) * 64],
                               r=xnT[:, hb * CH + ck * 512: hb * CH + (ck + 1) * 512],
                               st=(hb == 0), sp=(hb == NHB - 1):
                               t.matmul(o, l, r, start=st, stop=sp),
                               reads=[b_at, b_xnT[ck]], writes=[b_midp],
                               mark=(hb == NHB - 1))
                sch.op("ve", lambda v, o=mid2_sb[0:64, c0:c0 + CH], a=midp[0:64, :], bb=g4_bf[:]:
                       v.tensor_tensor(o, a, bb, ALU.mult),
                       reads=[b_midp, b_g4], writes=[b_mid2])
                sch.fence_psum(b_g4p, b_midp)


            with ExitStack() as pqk:
                sqp = [pqk.enter_context(nc.psum_tensor(f"sqp{c}_{i}", [128, CH], F32)) for i in range(2)]
                rot_ps = pqk.enter_context(nc.psum_tensor(f"rotp{c}", [128, CH], F32))
                b_sqp = [sch.new_psum_buf(f"sqp{i}") for i in range(2)]
                b_rotp = sch.new_psum_buf("rotp")
                if c == 0:
                    pre_ps = [pqk.enter_context(nc.psum_tensor(f"pre{i}", [128, 2 * SUB], F32)) for i in range(2)]
                    b_pre = [sch.new_psum_buf(f"pre{i}") for i in range(2)]
                pend_rot = []
                for obi in range(8, len(ob_order)):
                    ob = ob_order[obi]; sl = obi % 3
                    ps = obi % 2
                    for ck in range(CH // 512):
                        for hb in range(NHB):
                            sch.op("pe", lambda t, o=sqp[ps][:, ck * 512:(ck + 1) * 512],
                                   l=wg[sl][:, hb * 128:(hb + 1) * 128],
                                   r=xnT[:, hb * CH + ck * 512: hb * CH + (ck + 1) * 512],
                                   st=(hb == 0), sp=(hb == NHB - 1):
                                   t.matmul(o, l, r, start=st, stop=sp),
                                   reads=[b_wg[sl], b_xnT[ck]], writes=[b_sqp[ps]],
                                   mark=(hb == NHB - 1))
                    sch.op("sc", lambda s, o=qraw[ps][:], i=sqp[ps][:]: s.copy(o, i),
                           reads=[b_sqp[ps]], writes=[b_qraw[ps]])

                    def rot_chain(ps=ps, ob=ob, c0=c0):
                        for ck in range(CH // 512):
                            sch.op("pe", lambda t, o=rot_ps[:, ck * 512:(ck + 1) * 512],
                                   l=prot, r=qraw[ps][:, ck * 512:(ck + 1) * 512]:
                                   t.matmul(o, l, r, start=True, stop=True),
                                   reads=[b_const, b_qraw[ps]], writes=[b_rotp],
                                   mark=(ck == CH // 512 - 1))
                        sch.op("sc", lambda s, o=rotb[ps][:], i=rot_ps[:]: s.copy(o, i),
                               reads=[b_rotp], writes=[b_rotb[ps]])
                        sch.op("ve", lambda v, o=rotb[ps][:], a=rotb[ps][:], bb=trig_sb[:, 2048 + c0:2048 + c0 + CH]:
                               v.tensor_tensor(o, a, bb, ALU.mult),
                               reads=[b_rotb[ps], b_trig], writes=[b_rotb[ps]])
                        sch.op("ve", lambda v, o=qsp[ps][:], a=qraw[ps][:], bb=trig_sb[:, c0:c0 + CH]:
                               v.tensor_tensor(o, a, bb, ALU.mult),
                               reads=[b_qraw[ps], b_trig], writes=[b_qsp[ps]])
                        sch.op("ve", lambda v, o=qsp[ps][:], a=qsp[ps][:], bb=rotb[ps][:]:
                               v.tensor_tensor(o, a, bb, ALU.add),
                               reads=[b_rotb[ps], b_qsp[ps]], writes=[b_qsp[ps]])
                        if ob < 32:
                            dst, dn = QTs, "QTs"
                            off = ob * 128 * S + c0
                        else:
                            dst, dn = KTs, "KTs"
                            off = (ob - 32) * 128 * S + c0
                        out_ap = bass.AP(dst, off, [[S, 128], [1, CH]])
                        sch.dma("sy", f"spQ{ps}",
                                lambda s, o=out_ap, i=qsp[ps][:]: s.dma_start(o, i),
                                reads=[b_qsp[ps]], writes=[dbuf[dn]])
                    while pend_rot:
                        pend_rot.pop(0)()
                    pend_rot.append(rot_chain)
                    if obi + 3 < len(ob_order):
                        rec_wdma(obi + 3)
                    if c == 0:
                        if obi == 36: rec_xch(0, CH)
                        elif obi == 38: rec_xch(1, CH)
                        elif obi == 40: rec_xch(2, CH)
                        elif obi == 39: rec_sq(0)
                        elif obi == 41: rec_sq(1)
                        elif obi == 42: rec_mms(0, pre_ps[0], b_pre[0])
                        elif obi == 43: rec_mms(1, pre_ps[1], b_pre[1])
                        elif obi == 45: rec_chain(0, pre_ps[0], b_pre[0])
                        elif obi == 46: rec_chain(1, pre_ps[1], b_pre[1])
                while pend_rot:
                    pend_rot.pop(0)()
                sch.fence_psum(b_sqp[0], b_sqp[1], b_rotp, *( [b_pre[0], b_pre[1]] if c == 0 else []))

        if debug:
            dap = bass.AP(mid2_dump, 0, [[S, 64], [1, S]])
            sch.dma("sy", "dbgM", lambda s, o=dap, i=mid2_sb[0:64, :]:
                    s.dma_start(o, i), reads=[b_mid2], writes=[_Buf("dmpm")])
        sch.barrier()

    with ExitStack() as p23:
        OTr = p23.enter_context(nc.sbuf_tensor("OTr", [128, NQH * S], BF16))
        bcat_sb = p23.enter_context(nc.sbuf_tensor("bcat_sb", [64, 4096], BF16))
        wo = [p23.enter_context(nc.sbuf_tensor(f"wo{i}", [128, 4096], BF16)) for i in range(2)]
        b_OTr = _Buf("OTr")
        b_wo = [_Buf(f"wo{i}") for i in range(2)]
        sch.dma("gd", "c4", lambda g, o=bcat_sb[:], i=bc_in.ap(): g.dma_start(o, i), writes=[b_bcat])

        def rec_wo_dma(nb):
            sl = nb % 2
            woap = bass.AP(Wo_blk, nb * 128 * 4096, [[4096, 128], [1, 4096]])
            sch.dma("gd", f"wo{sl}", lambda g, o=wo[sl][:], i=woap:
                    g.dma_start(o, i), writes=[b_wo[sl]])

        rec_wo_dma(0); rec_wo_dma(1)

        with ExitStack() as p2:
            ktg = [p2.enter_context(nc.sbuf_tensor(f"ktg{i}", [128, S], BF16)) for i in range(2)]
            vg = [p2.enter_context(nc.sbuf_tensor(f"vg{i}", [128, S], BF16)) for i in range(2)]
            qt = [p2.enter_context(nc.sbuf_tensor(f"qt{i}", [128, TT], BF16)) for i in range(4)]
            pt_sb = [p2.enter_context(nc.sbuf_tensor(f"pt{i}", [128, 2 * TT], BF16)) for i in range(4)]
            recf = p2.enter_context(nc.sbuf_tensor("recf", [1, TT], F32))
            recb = [p2.enter_context(nc.sbuf_tensor(f"recb{i}", [1, TT], BF16)) for i in range(2)]
            rb_sb = p2.enter_context(nc.sbuf_tensor("rb_sb", [128, TT], F32))

            pt_ps = [p2.enter_context(nc.psum_tensor(f"ptp{i}", [128, 2 * TT], F32)) for i in range(2)]
            ot_ps = [p2.enter_context(nc.psum_tensor(f"otp{i}", [128, TT], F32)) for i in range(2)]
            denrb_ps = [p2.enter_context(nc.psum_tensor(f"denrb{i}", [128, TT], F32)) for i in range(2)]

            b_ktg = [_Buf(f"ktg{i}") for i in range(2)]
            b_vg = [_Buf(f"vg{i}") for i in range(2)]
            b_qt = [_Buf(f"qt{i}") for i in range(4)]
            b_ptsb = [_Buf(f"pt{i}") for i in range(4)]
            b_recf = _Buf("recf"); b_recb = [_Buf(f"recb{i}") for i in range(2)]; b_rbsb = _Buf("rb_sb")
            b_ptps = [sch.new_psum_buf(f"ptp{i}") for i in range(2)]
            b_otps = [sch.new_psum_buf(f"otp{i}") for i in range(2)]
            b_denrb = [sch.new_psum_buf(f"denrb{i}") for i in range(2)]

            def rec_kv_dma(g):
                sl = g % 2
                kin = bass.AP(KTs, g * 128 * S, [[S, 128], [1, S]])
                sch.dma("sy", f"ktg{sl}", lambda s, o=ktg[sl][:], i=kin: s.dma_start(o, i),
                        reads=[dbuf["KTs"]], writes=[b_ktg[sl]])
                vin = bass.AP(Vs, g * 128, [[KVO, 128], [128 * KVO, S // 128], [1, 128]])
                sch.dma("sy", f"vg{sl}", lambda s, o=vg[sl][:], i=vin: s.dma_start(o, i),
                        reads=[dbuf["Vs"]], writes=[b_vg[sl]])

            rec_kv_dma(0)
            pending_tail = []
            for g in range(NKV):
                gsl = g % 2
                if g + 1 < NKV:
                    rec_kv_dma(g + 1)
                iters = [(g * 4 + 2 * hp + st_, tile)
                         for hp in range(2) for tile in range(NTILE) for st_ in range(2)]

                def rec_qt(i):
                    hh, tt_ = iters[i]
                    qsl = i % 4
                    qin = bass.AP(QTs, hh * 128 * S + tt_ * TT, [[S, 128], [1, TT]])
                    sch.dma("sy", f"qt{qsl}", lambda s, o=qt[qsl][:], i=qin: s.dma_start(o, i),
                            reads=[dbuf["QTs"]], writes=[b_qt[qsl]])

                rec_qt(0); rec_qt(1)
                it = 0
                for hp in range(2):
                    for tile in range(NTILE):
                        heads = (g * 4 + 2 * hp, g * 4 + 2 * hp + 1)
                        qsl2 = (it % 4, (it + 1) % 4)
                        if it + 2 < len(iters):
                            rec_qt(it + 2)
                        if it + 3 < len(iters):
                            rec_qt(it + 3)
                        t0 = tile * TT
                        nsb = t0 // 128 + 4
                        npair = nsb // 2

                        def rec_scores(j, st_):
                            par = st_
                            for half in range(2):
                                sb = 2 * j + half
                                vis0 = max(0, sb * 128 - t0)
                                sch.op("pe", lambda t,
                                       o=pt_ps[par][:, half * TT + vis0: (half + 1) * TT],
                                       l=ktg[gsl][:, sb * 128:(sb + 1) * 128],
                                       r=qt[qsl2[st_]][:, vis0:TT]:
                                       t.matmul(o, l, r, start=True, stop=True),
                                       reads=[b_ktg[gsl], b_qt[qsl2[st_]]], writes=[b_ptps[par]])

                        def rec_exp(j, st_, slot):
                            par = st_
                            vis = [max(0, (2 * j + half) * 128 - t0) for half in range(2)]
                            if vis[0] == 0 and vis[1] == 0:
                                sch.op("sc", lambda s, o=pt_sb[slot][:, 0:2 * TT],
                                       i=pt_ps[par][:, 0:2 * TT]:
                                       s.activation(o, i, AF.Exp, bias=SHIFT_c, scale=ISD_c),
                                       reads=[b_ptps[par], b_cf], writes=[b_ptsb[slot]])
                            else:
                                for half in range(2):
                                    v0 = half * TT + vis[half]
                                    sch.op("sc", lambda s, o=pt_sb[slot][:, v0:(half + 1) * TT],
                                           i=pt_ps[par][:, v0:(half + 1) * TT]:
                                           s.activation(o, i, AF.Exp, bias=SHIFT_c, scale=ISD_c),
                                           reads=[b_ptps[par], b_cf], writes=[b_ptsb[slot]])
                            for half in range(2):
                                sb = 2 * j + half
                                if sb * 128 >= t0:
                                    v0 = half * TT + vis[half]
                                    sch.op("ve", lambda v, o=pt_sb[slot][:, v0:v0 + 128],
                                           a=pt_sb[slot][:, v0:v0 + 128], bb=mask01:
                                           v.tensor_tensor(o, a, bb, ALU.mult),
                                           reads=[b_ptsb[slot], b_const], writes=[b_ptsb[slot]])

                        def rec_pv(j, st_, slot):
                            vis = [max(0, (2 * j + half) * 128 - t0) for half in range(2)]
                            first = (j == 0)
                            last = (j == npair - 1)
                            for half in range(2):
                                sb = 2 * j + half
                                v0 = vis[half]
                                sch.op("pe", lambda t, o=ot_ps[st_][:, v0:TT],
                                       l=vg[gsl][:, sb * 128:(sb + 1) * 128],
                                       r=pt_sb[slot][:, half * TT + v0:(half + 1) * TT],
                                       st=(first and half == 0), sp=(last and half == 1):
                                       t.matmul(o, l, r, start=st, stop=sp),
                                       reads=[b_vg[gsl], b_ptsb[slot]], writes=[b_otps[st_]])
                            for half in range(2):
                                sb = 2 * j + half
                                v0 = vis[half]
                                sch.op("pe", lambda t, o=denrb_ps[st_][0:1, v0:TT],
                                       l=ones_col,
                                       r=pt_sb[slot][:, half * TT + v0:(half + 1) * TT],
                                       st=(first and half == 0), sp=(last and half == 1):
                                       t.matmul(o, l, r, start=st, stop=sp),
                                       reads=[b_const, b_ptsb[slot]], writes=[b_denrb[st_]])

                        def slot_of(j, st_):
                            return (2 * j + st_) % 4

                        for j in range(npair):
                            rec_scores(j, 0)
                            rec_scores(j, 1)
                            while pending_tail:
                                pending_tail.pop(0)()
                            rec_exp(j, 0, slot_of(j, 0))
                            rec_exp(j, 1, slot_of(j, 1))
                            if j >= 1:
                                rec_pv(j - 1, 0, slot_of(j - 1, 0))
                                rec_pv(j - 1, 1, slot_of(j - 1, 1))
                        rec_pv(npair - 1, 0, slot_of(npair - 1, 0))
                        rec_pv(npair - 1, 1, slot_of(npair - 1, 1))

                        def rec_tail(h_=heads, t0=t0):
                            for st_ in range(2):
                                rf = recf[0:1, 0:TT]
                                sch.op("sc", lambda s, o=rf, i=denrb_ps[st_][0:1, :], st_=st_:
                                       s.activation(o, i, AF.Ln), reads=[b_denrb[st_]], writes=[b_recf])
                                sch.op("sc", lambda s, o=recb[st_][0:1, :], i=rf, st_=st_:
                                       s.activation(o, i, AF.Exp, scale=NEG1_c),
                                       reads=[b_recf, b_cf], writes=[b_recb[st_]])
                                sch.op("pe", lambda t, o=denrb_ps[st_][:], l=ones_row,
                                       r=recb[st_][0:1, :], st_=st_:
                                       t.matmul(o, l, r, start=True, stop=True),
                                       reads=[b_const, b_recb[st_]], writes=[b_denrb[st_]])
                                sch.op("ve", lambda v, o=rb_sb[:], i=denrb_ps[st_][:], st_=st_:
                                       v.tensor_copy(o, i),
                                       reads=[b_denrb[st_]], writes=[b_rbsb])
                                sch.op("ve", lambda v, o=OTr[:, h_[st_] * S + t0: h_[st_] * S + t0 + TT],
                                       a=ot_ps[st_][:], bb=rb_sb[:], st_=st_:
                                       v.tensor_tensor(o, a, bb, ALU.mult),
                                       reads=[b_otps[st_], b_rbsb], writes=[b_OTr])
                        pending_tail.append(rec_tail)
                        it += 2
            while pending_tail:
                pending_tail.pop(0)()
            sch.fence_psum(*(b_ptps + b_otps + b_denrb))
            if debug:
                for h in range(NQH):
                    oap = bass.AP(OT_dump, h * 128 * S, [[S, 128], [1, S]])
                    sch.dma("sy", "dbgO", lambda s, o=oap, i=OTr[:, h * S:(h + 1) * S]:
                            s.dma_start(o, i), reads=[b_OTr], writes=[_Buf("dump")])

        sch.barrier()

        with ExitStack() as p3:
            xt3 = [p3.enter_context(nc.sbuf_tensor(f"xt3{i}", [128, S], F32)) for i in range(2)]
            gt3 = [p3.enter_context(nc.sbuf_tensor(f"gt3{i}", [128, CH], BF16)) for i in range(2)]
            outst = [p3.enter_context(nc.sbuf_tensor(f"outst{i}", [128, CH], F32)) for i in range(2)]
            st_ps = [p3.enter_context(nc.psum_tensor(f"stp{i}", [128, CH], F32)) for i in range(2)]
            dl_ps = [p3.enter_context(nc.psum_tensor(f"dlp{i}", [128, CH], F32)) for i in range(2)]
            b_xt3 = [_Buf(f"xt3{i}") for i in range(2)]
            b_gt3 = [_Buf(f"gt3{i}") for i in range(2)]
            b_outst = [_Buf(f"outst{i}") for i in range(2)]
            b_stps = [sch.new_psum_buf(f"stp{i}") for i in range(2)]
            b_dlps = [sch.new_psum_buf(f"dlp{i}") for i in range(2)]

            def rec_p3_dma(nb, skip_wo=False):
                if not skip_wo:
                    rec_wo_dma(nb)
                sl = nb % 2
                xin = bass.AP(xT, nb * 128 * S, [[S, 128], [1, S]])
                sch.dma("sy", f"xt3{sl}", lambda s, o=xt3[sl][:], i=xin: s.dma_start(o, i),
                        reads=[dbuf["xT"]], writes=[b_xt3[sl]])

            rec_p3_dma(0, skip_wo=True); rec_p3_dma(1, skip_wo=True)
            it3 = 0
            for nb in range(32):
                sl = nb % 2
                for tc in range(S // CH):
                    ps = it3 % 2
                    it3 += 1
                    for ob in range(32):
                        for ck in range(CH // 512):
                            sch.op("pe", lambda t, o=st_ps[ps][:, ck * 512:(ck + 1) * 512],
                                   l=wo[sl][:, ob * 128:(ob + 1) * 128],
                                   r=OTr[:, ob * S + tc * CH + ck * 512: ob * S + tc * CH + (ck + 1) * 512],
                                   st=(ob == 0), sp=(ob == 31):
                                   t.matmul(o, l, r, start=st, stop=sp),
                                   reads=[b_wo[sl], b_OTr], writes=[b_stps[ps]],
                                   mark=(ob == 31))
                    for ck in range(CH // 512):
                        sch.op("pe", lambda t, o=dl_ps[ps][:, ck * 512:(ck + 1) * 512],
                               l=bcat_sb[0:64, nb * 128:(nb + 1) * 128],
                               r=mid2_sb[0:64, tc * CH + ck * 512: tc * CH + (ck + 1) * 512]:
                               t.matmul(o, l, r, start=True, stop=True),
                               reads=[b_bcat, b_mid2], writes=[b_dlps[ps]],
                               mark=(ck == CH // 512 - 1))
                    sch.op("sc", lambda s, o=gt3[ps][:], i=dl_ps[ps][:]:
                           s.activation(o, i, AF.Silu),
                           reads=[b_dlps[ps]], writes=[b_gt3[ps]])
                    sch.op("ve", lambda v, o=outst[ps][:], a=st_ps[ps][:], bb=gt3[ps][:]:
                           v.tensor_tensor(o, a, bb, ALU.mult),
                           reads=[b_stps[ps], b_gt3[ps]], writes=[b_outst[ps]])
                    sch.op("ve", lambda v, o=outst[ps][:], a=outst[ps][:],
                           bb=xt3[sl][:, tc * CH:(tc + 1) * CH]:
                           v.tensor_tensor(o, a, bb, ALU.add),
                           reads=[b_outst[ps], b_xt3[sl]], writes=[b_outst[ps]])
                    oap = bass.AP(outT, nb * 128 * S + tc * CH, [[S, 128], [1, CH]])
                    sch.dma("sy", f"so{ps}", lambda s, o=oap, i=outst[ps][:]:
                            s.dma_start(o, i), reads=[b_outst[ps]], writes=[dbuf["outT"]])
                if nb + 2 < 32:
                    rec_p3_dma(nb + 2)
            sch.fence_psum(*(b_stps + b_dlps))

    sch.barrier()

    with nc.Block() as block:
        semstack = sch.emit(nc, block)
    semstack.close()
    st.close()
    return nc



def _host_shared(Wq, Wk, Wv, Wo, A, Bm, Wr):
    f32 = np.float32
    WcatT = np.concatenate([Wq.T, Wk.T, Wv.T], axis=1).astype(f32)
    Wblk = np.ascontiguousarray(
        WcatT.reshape(32, 128, 48, 128).transpose(2, 1, 0, 3).reshape(48, 128, 4096))
    WoT = Wo.T.astype(f32)
    Wo_blk = np.ascontiguousarray(
        WoT.reshape(32, 128, 32, 128).transpose(2, 1, 0, 3).reshape(32, 128, 4096))
    wr_blk = np.ascontiguousarray(
        Wr.T.astype(f32).reshape(32, 128, 4).transpose(1, 0, 2).reshape(128, 128))
    a_blk = np.ascontiguousarray(
        A.transpose(2, 0, 1).reshape(4096, 64).astype(f32)
        .reshape(32, 128, 64).transpose(1, 0, 2).reshape(128, 2048))
    bcat = np.ascontiguousarray(Bm.transpose(0, 2, 1).reshape(64, 4096).astype(f32))

    consts = np.zeros((128, 576), f32)
    consts[:, 0:128] = np.eye(128, dtype=f32)
    prot = np.zeros((128, 128), f32)
    for d in range(64):
        prot[d + 64, d] = -1.0
        prot[d, d + 64] = 1.0
    consts[:, 128:256] = prot
    m01 = np.zeros((128, 128), f32)
    for s_ in range(128):
        m01[s_, s_:] = 1.0
    consts[:, 256:384] = m01
    e4 = np.zeros((128, 64), f32)
    for e in range(4):
        e4[e, e * 16:(e + 1) * 16] = 4.0
    consts[:, 384:448] = e4
    consts[:, 448:576] = 1.0
    cf32 = np.zeros((128, 8), f32)
    cf32[:, 0] = 1e-5
    cf32[:, 1] = -0.5
    cf32[:, 2] = 1.0 / math.sqrt(128.0)
    cf32[:, 3] = -10.0
    cf32[:, 4] = 1.0 / 4096.0
    cf32[:, 5] = -1.0
    cf32[:, 6] = -1e9

    freqs = np.exp(-np.arange(0, 128, 2, dtype=f32) * (math.log(10000.0) / 128)).astype(f32)
    ang = np.arange(S, dtype=f32)[:, None] * freqs[None, :]
    cosT = np.cos(ang).T.astype(f32)
    sinT = np.sin(ang).T.astype(f32)
    trig = np.zeros((128, 4096), f32)
    trig[0:64, 0:S] = cosT
    trig[64:128, 0:S] = cosT
    trig[0:64, 2048:2048 + S] = sinT
    trig[64:128, 2048:2048 + S] = sinT
    return dict(Wblk=Wblk, Wo_blk=Wo_blk, wr_blk=wr_blk, a_blk=a_blk,
                bcat=bcat, consts=consts, cf32=cf32, trig=trig)


def kernel(L0_attn_hidden_states, L0_attn_Wq, L0_attn_Wk, L0_attn_Wv, L0_attn_Wo,
           L0_ffn_A, L0_ffn_B, L0_ffn_Wr):
    x = np.asarray(L0_attn_hidden_states, dtype=np.float32)
    shared = _host_shared(np.asarray(L0_attn_Wq), np.asarray(L0_attn_Wk),
                          np.asarray(L0_attn_Wv), np.asarray(L0_attn_Wo),
                          np.asarray(L0_ffn_A), np.asarray(L0_ffn_B),
                          np.asarray(L0_ffn_Wr))
    nc = build_nc()
    in_maps = []
    for i in range(8):
        m = {"xT": np.ascontiguousarray(x[i].T)}
        m.update(shared)
        in_maps.append(m)
    res = run_bass_kernel_spmd(nc, in_maps, core_ids=list(range(8)))
    out = np.empty((8, S, H), dtype=np.float32)
    for i in range(8):
        out[i] = res.results[i]["outT"].T
    return out
